# version 14
# baseline (speedup 1.0000x reference)
import contextlib
import numpy as np
import concourse.bass as bass
import concourse.mybir as mybir

F32 = mybir.dt.float32
BF16 = mybir.dt.bfloat16
AF = mybir.ActivationFunctionType
ALU = mybir.AluOpType
AX = mybir.AxisListType

D = 1024
T = 2048
ST = 512
EPS_ATTN = 1e-6
EPS_LN = 1e-5
CAP = 100
FCH = 1024
RELAX_NS = 1


def am_off(h):
    hc = h // 8
    slot = (h % 8) // 2
    return hc * 1024 + (h % 2) * 512 + slot * 128


class Prog:
    ENGS = ("sync", "gpsimd", "tensor", "vector", "scalar")

    def __init__(self):
        self.items = {e: [] for e in self.ENGS}
        self.counts = {}

    def sem(self, name):
        self.counts.setdefault(name, 0)
        return name

    def add(self, name, n=1):
        self.counts[name] = self.counts.get(name, 0) + n
        return self.counts[name]

    def cnt(self, name):
        return self.counts.get(name, 0)

    def emit(self, eng, fn):
        self.items[eng].append(fn)


def build_program(nc, BPC, n_cores, debug=False):
    NT = BPC * T
    NST = NT // ST
    NTT = NT // 128
    STB = T // ST
    CCH = T // 128
    NFC = max(1, NT // FCH)

    x_in = nc.dram_tensor("x", [NT, D], F32, kind="ExternalInput").ap()
    w_in = {k: nc.dram_tensor(f"w{k}", [D, D], F32, kind="ExternalInput").ap()
            for k in ("q", "k", "v", "o", "e")}
    wg_in = nc.dram_tensor("wg", [D, 8], F32, kind="ExternalInput").ap()
    be_in = nc.dram_tensor("be", [1, D], F32, kind="ExternalInput").ap()
    sel_in = nc.dram_tensor("sel", [8 * n_cores, 8], F32, kind="ExternalInput").ap()
    y_out = nc.dram_tensor("y", [NT, D], F32, kind="ExternalOutput").ap()

    dbg = "ExternalOutput" if debug else "Internal"
    xbf_h = nc.dram_tensor("xbf_h", [NT, D], BF16, kind="Internal").ap()
    wbf_h = {k: nc.dram_tensor(f"w{k}_h", [D, D], BF16, kind="Internal").ap()
             for k in ("q", "k", "v", "o", "e")}
    kfT_h = nc.dram_tensor("kfT_h", [D, NT], BF16, kind=dbg).ap()
    ao_h = nc.dram_tensor("ao_h", [NT, D], BF16, kind=dbg).ap()
    x1_h = nc.dram_tensor("x1_h", [NT, D], F32, kind=dbg).ap()
    x1hi_h = nc.dram_tensor("x1hi_h", [NT, D], BF16, kind="Internal").ap()
    x1lo_h = nc.dram_tensor("x1lo_h", [NT, D], BF16, kind="Internal").ap()
    bcol_h = nc.dram_tensor("bcol_h", [128, 128], BF16, kind=dbg).ap()
    gcol_h = nc.dram_tensor("gcol_h", [128, 128], BF16, kind=dbg).ap()
    btok_h = nc.dram_tensor("btok_h", [128, 128], BF16, kind=dbg).ap()
    gtok_h = nc.dram_tensor("gtok_h", [128, 128], BF16, kind="Internal").ap()
    g_h = nc.dram_tensor("g_h", [1, NT], F32, kind=dbg).ap()
    dbg_h = nc.dram_tensor("dbg_h", [128, 16 * 4 + 520 + 2048 + 128], F32, kind=dbg).ap() if debug else None
    cc_in = nc.dram_tensor("cc_in", [8, 1], F32, kind="Internal").ap()
    cc_out = nc.dram_tensor("cc_out", [8 * n_cores, 1], F32, kind="Internal").ap()

    es = contextlib.ExitStack()
    es1 = contextlib.ExitStack()
    E = es.enter_context
    E1 = es1.enter_context
    wT = {k: E(nc.sbuf_tensor(f"wT{k}", [128, 8 * D], BF16)) for k in ("q", "k", "v", "o")}
    wg_f = E(nc.sbuf_tensor("wg_f", [128, 8 * 8], F32))
    wg_hl = E(nc.sbuf_tensor("wg_hl", [128, 8 * 16], BF16))
    be_f = E(nc.sbuf_tensor("be_f", [1, D], F32))
    be_b = E(nc.sbuf_tensor("be_b", [1, D], BF16))
    sel_sb = E(nc.sbuf_tensor("sel_sb", [8 * n_cores, 8], F32))
    ones1b = E(nc.sbuf_tensor("ones1b", [1, 128], BF16))
    onescol = E(nc.sbuf_tensor("onescol", [128, 1], BF16))
    ones8 = E(nc.sbuf_tensor("ones8", [8, 1], F32))
    cst_eps = E(nc.sbuf_tensor("cst_eps", [128, 1], F32))
    cst_mh = E(nc.sbuf_tensor("cst_mh", [128, 1], F32))
    iota8 = E(nc.sbuf_tensor("iota8", [8, 1], F32))
    iota8r = E(nc.sbuf_tensor("iota8r", [128, 8], F32))
    iotap = E(nc.sbuf_tensor("iotap", [128, 1], F32))
    mask4 = E(nc.sbuf_tensor("mask4", [128, 512], BF16))
    recip = E(nc.sbuf_tensor("recip", [128, 16], F32))
    dtmp = E(nc.sbuf_tensor("dtmp", [128, 16], F32))
    jk2 = E(nc.sbuf_tensor("jk2", [128, 4], F32))
    vjk = E(nc.sbuf_tensor("vjk", [128, 1], F32))
    xres = [E(nc.sbuf_tensor(f"xres{j}", [128, D], F32)) for j in range(2)]
    ubuf = [E(nc.sbuf_tensor(f"ubuf{j}", [128, D], F32)) for j in range(2)]
    x1b = [E(nc.sbuf_tensor(f"x1b{j}", [128, D], F32)) for j in range(2)]
    x1hi = [E(nc.sbuf_tensor(f"x1hi{j}", [128, D], BF16)) for j in range(2)]
    x1lo = [E(nc.sbuf_tensor(f"x1lo{j}", [128, D], BF16)) for j in range(2)]
    stat = [E(nc.sbuf_tensor(f"stat{j}", [128, 12], F32)) for j in range(2)]
    mv = [E(nc.sbuf_tensor(f"mv{j}", [128, 2], F32)) for j in range(2)]
    lnt = [E(nc.sbuf_tensor(f"lnt{j}", [128, 4], F32)) for j in range(2)]
    hiT = E(nc.sbuf_tensor("hiT", [128, 8 * ST], BF16))
    loT = E(nc.sbuf_tensor("loT", [128, 8 * ST], BF16))
    lgb = E(nc.sbuf_tensor("lgb", [128, 8], F32))
    lge = E(nc.sbuf_tensor("lge", [128, 8], F32))
    ohb = E(nc.sbuf_tensor("ohb", [128, 8], F32))
    smt = E(nc.sbuf_tensor("smt", [128, 3], F32))
    gate_c = E(nc.sbuf_tensor("gate_c", [128, NTT], F32))
    best_c = E(nc.sbuf_tensor("best_c", [128, NTT], F32))
    gate_cb = E(nc.sbuf_tensor("gate_cb", [128, 128], BF16))
    best_cb = E(nc.sbuf_tensor("best_cb", [128, 128], BF16))
    xT = [E1(nc.sbuf_tensor(f"xT{j}", [128, 8 * ST], BF16)) for j in range(2)]
    qfT = E1(nc.sbuf_tensor("qfT", [128, 8 * ST], BF16))
    kfT = E1(nc.sbuf_tensor("kfT", [128, 8 * ST], BF16))
    vbuf = E1(nc.sbuf_tensor("vbuf", [128, 4 * 1040], BF16))
    vaug = vbuf.ap().rearrange("p (c h v) -> p c h v", h=16, v=65)
    kf = [E1(nc.sbuf_tensor(f"kf{j}", [128, D], BF16)) for j in range(2)]
    spE = E1(nc.sbuf_tensor("spE", [128, 1024], F32))
    spL = E1(nc.sbuf_tensor("spL", [128, 1024], F32))
    iotac = spE[:, 0:512]
    mask4f = spL[:, 0:512]
    amT = [E1(nc.sbuf_tensor(f"amT{j}", [128, 2048], BF16)) for j in range(2)]
    kv_sb = E1(nc.sbuf_tensor("kv_sb", [128, 520], BF16))
    aob = E1(nc.sbuf_tensor("aob", [128, 4 * D], BF16))
    aoT = E1(nc.sbuf_tensor("aoT", [128, 8 * ST], BF16))
    B = [E(nc.psum_tensor(f"pb{j}", [128, 512], F32)) for j in range(8)]

    p = Prog()
    S = {}

    for nm in ("GW", "LW", "GPRE", "PRE", "LXT0", "LXT1", "MMF", "BF2", "BF3",
               "BF4", "BF5", "BF6", "BF7", "AEX", "LG4",
               "VEQ", "GKF", "LKF0", "LKF1", "CA", "MEV", "NSC", "RCP", "DIV", "KVE",
               "GAO", "LAO", "LXR0", "LXR1", "VLN", "ALN", "GX1", "LHT", "VLG",
               "VTL", "GTL", "LTL", "CCD", "GP2", "LP2", "VP2", "AP2", "GY", "GDB", "DBV"):
        p.sem(nm)
    for b in range(BPC):
        p.sem(f"GWX{b}")

    bank = {2: {"frees": 0}, 3: {"frees": 0}, 6: {"frees": 0}, 7: {"frees": 0}}

    def g_pre(g):
        for k in ("q", "k", "v", "o", "e"):
            g.dma_start(out=wbf_h[k][:, :], in_=w_in[k][:, :]).then_inc(S["GW"], 16)
        for b in range(BPC):
            g.dma_start(out=xbf_h[b * T:(b + 1) * T, :],
                        in_=x_in[b * T:(b + 1) * T, :]).then_inc(S[f"GWX{b}"], 16)
        g.iota(iota8[:, :], pattern=[[0, 1]], channel_multiplier=1,
               allow_small_or_imprecise_dtypes=True)
        g.iota(iota8r[:, :], pattern=[[1, 8]], channel_multiplier=0,
               allow_small_or_imprecise_dtypes=True)
        g.iota(iotap[:, :], pattern=[[0, 1]], channel_multiplier=1,
               allow_small_or_imprecise_dtypes=True)
        g.iota(iotac, pattern=[[0, 4], [1, 128]], channel_multiplier=0,
               allow_small_or_imprecise_dtypes=True)
        g.memset(ones1b[:, :], 1.0)
        g.memset(onescol[:, :], 1.0)
        g.memset(vaug[:, :, :, 64:65], 1.0)
        g.memset(gate_cb[:, :], 0.0)
        g.memset(cst_eps[:, :], EPS_LN)
        g.memset(cst_mh[:, :], -0.5)
        g.memset(best_cb[:, :], 0.0)
        g.memset(ones8[:, :], 1.0).then_inc(S["GPRE"], 1)
    p.emit("gpsimd", g_pre)
    p.add("GW", 16 * 5)
    for b in range(BPC):
        p.add(f"GWX{b}", 16)
    p.add("GPRE", 1)

    def s_pre(sy):
        sy.wait_ge(S["GW"], 16 * 5)
        for k in ("q", "k", "v", "o"):
            for kk in range(8):
                sy.dma_start(out=wT[k][:, kk * D:(kk + 1) * D],
                             in_=wbf_h[k][:, kk * 128:(kk + 1) * 128],
                             transpose=True).then_inc(S["LW"], 16)
        sy.dma_start(out=wg_f.ap().rearrange("p (kk e) -> p kk e", kk=8),
                     in_=wg_in.rearrange("(kk p) e -> p kk e", p=128)).then_inc(S["LW"], 16)
        sy.dma_start(out=be_f[:, :], in_=be_in[:, :]).then_inc(S["LW"], 16)
        sy.dma_start(out=sel_sb[:, :], in_=sel_in[:, :]).then_inc(S["LW"], 16)
    p.emit("sync", s_pre)
    p.add("LW", 16 * 35)

    def v_pre(v):
        v.wait_ge(S["GPRE"], 1)
        v.wait_ge(S["LW"], 16 * 35)
        v.tensor_single_scalar(mask4f, iotac, iotap[:, :], ALU.is_ge)
        v.tensor_copy(out=vjk[:, :], in_=iotap[:, :])
        v.tensor_copy(out=mask4[:, :], in_=mask4f)
        for kk in range(8):
            v.tensor_copy(out=wg_hl[:, kk * 16:kk * 16 + 8], in_=wg_f[:, kk * 8:(kk + 1) * 8])
        v.tensor_copy(out=vjk[:, :], in_=iotap[:, :])
        for kk in range(8):
            v.tensor_sub(wg_hl[:, kk * 16 + 8:kk * 16 + 16], wg_f[:, kk * 8:(kk + 1) * 8],
                         wg_hl[:, kk * 16:kk * 16 + 8])
        v.tensor_copy(out=be_b[:, :], in_=be_f[:, :]).then_inc(S["PRE"], 1)
    p.emit("vector", v_pre)
    p.add("PRE", 1)

    lxt_after = {}
    xt_cons = {}
    veq_after = {}
    qk_n = [0]
    vfree_after = {}
    lkf_after = {}
    mev_after = {}
    nsc_after = {}
    rcp_after = {}
    div_after = {}
    mev_mid_t = {}
    kve_after = {}
    gao_after = {}
    lao_after = {}
    wo_cons = {}
    lxr_after = {}
    u_free = {}
    xres_free = {}
    x1_free = {}
    vlo_after = {}
    gx1_after = {}
    lht_after = {}
    lg_cons = {}
    lp2_st = {}
    lp2_x1 = {}
    eo_cons = {}
    gkf_prev = {}
    aln_lnexp = {}
    ap2_lnexp = {}
    p2_tile_free = {}
    vp2_st = {}
    gy_after = {}

    def stage_sHT(i):
        t0 = i * ST

        def s_ht(sy, i=i, t0=t0):
            sy.wait_ge(S["GX1"], gx1_after[i])
            if i >= 1:
                sy.wait_ge(S["MMF"], lg_cons[i - 1])
            for kk in range(8):
                sy.dma_start(out=hiT[:, kk * ST:(kk + 1) * ST],
                             in_=x1hi_h[t0:t0 + ST, kk * 128:(kk + 1) * 128],
                             transpose=True).then_inc(S["LHT"], 16)
                sy.dma_start(out=loT[:, kk * ST:(kk + 1) * ST],
                             in_=x1lo_h[t0:t0 + ST, kk * 128:(kk + 1) * 128],
                             transpose=True).then_inc(S["LHT"], 16)
        p.emit("sync", s_ht)
        lht_after[i] = p.add("LHT", 16 * 16)

    def stage_sXT(i):
        b = i // STB
        t0 = i * ST
        xTb = xT[i % 2]

        def s_xt(sy, i=i, b=b, t0=t0, xTb=xTb):
            sy.wait_ge(S[f"GWX{b}"], 16)
            if i >= 2:
                sy.wait_ge(S["MMF"], xt_cons[i - 2])
            for kk in range(8):
                sy.dma_start(out=xTb[:, kk * ST:(kk + 1) * ST],
                             in_=xbf_h[t0:t0 + ST, kk * 128:(kk + 1) * 128],
                             transpose=True).then_inc(S[f"LXT{i % 2}"], 16)
        p.emit("sync", s_xt)
        lxt_after[i] = p.add(f"LXT{i % 2}", 16 * 8)

    def stage_sAO(i):
        t0 = i * ST

        def s_ao(sy, i=i, t0=t0):
            sy.wait_ge(S["GAO"], gao_after[i])
            if i >= 1:
                sy.wait_ge(S["MMF"], wo_cons[i - 1])
            for kk in range(8):
                sy.dma_start(out=aoT[:, kk * ST:(kk + 1) * ST],
                             in_=ao_h[t0:t0 + ST, kk * 128:(kk + 1) * 128],
                             transpose=True).then_inc(S["LAO"], 16)
        p.emit("sync", s_ao)
        lao_after[i] = p.add("LAO", 16 * 8)

    def stage_fQ(i):
        b, stb = divmod(i, STB)
        t0 = i * ST
        xTb = xT[i % 2]

        if i <= 1:
            stage_sXT(i)
        if i >= 2:
            stage_sHT(i - 2)
        if i >= 1:
            stage_sAO(i - 1)

        prev_nsc = nsc_after.get(i * 4 - 1, 0)
        prev_gkf = gkf_prev.get(i - 1, 0)
        for (nm, dstT) in (("q", qfT), ("k", kfT)):
            for ee in range(8):
                bid = 2 + (ee % 2)
                bf = f"BF{bid}"
                w_bf = bank[bid]["frees"]
                my_fill = p.add("MMF", 1)

                def t_qk(t, nm=nm, ee=ee, bid=bid, w_bf=w_bf, xTb=xTb, i=i):
                    t.wait_ge(S[f"BF{bid}"], w_bf)
                    t.wait_ge(S[f"LXT{i % 2}"], lxt_after[i])
                    t.wait_ge(S["PRE"], 1)
                    for kk in range(8):
                        inst = t.matmul(B[bid][:, :],
                                        wT[nm][:, kk * D + ee * 128: kk * D + (ee + 1) * 128],
                                        xTb[:, kk * ST:(kk + 1) * ST],
                                        start=(kk == 0), stop=(kk == 7))
                    inst.then_inc(S["MMF"], 1)
                p.emit("tensor", t_qk)

                qk_n[0] += 1
                veq_wait = max(0, qk_n[0] - 2)
                sp0 = (qk_n[0] % 2) * 512
                my_aex = p.add("AEX", 1)

                def a_sp(s, my_fill=my_fill, bid=bid, sp0=sp0, veq_wait=veq_wait):
                    s.wait_ge(S["MMF"], my_fill)
                    if veq_wait > 0:
                        s.wait_ge(S["VEQ"], veq_wait)
                    s.activation(out=spE[:, sp0:sp0 + 512], in_=B[bid][:, :],
                                 func=AF.Exp).then_inc(S[f"BF{bid}"], 1)
                    s.activation(out=spL[:, sp0:sp0 + 512], in_=spE[:, sp0:sp0 + 512],
                                 func=AF.Ln, bias=1.0).then_inc(S["AEX"], 1)
                p.emit("scalar", a_sp)
                bank[bid]["frees"] += 1

                my_veq = p.add("VEQ", 1)

                def v_sp(v, my_aex=my_aex, dstT=dstT, ee=ee, sp0=sp0,
                         prev_nsc=prev_nsc, prev_gkf=prev_gkf, nm=nm):
                    v.wait_ge(S["AEX"], my_aex)
                    if ee == 0 and prev_nsc > 0:
                        v.wait_ge(S["NSC"], prev_nsc)
                        if nm == "k":
                            v.wait_ge(S["GKF"], prev_gkf)
                    v.tensor_scalar_add(dstT[:, ee * ST:(ee + 1) * ST],
                                        spL[:, sp0:sp0 + 512], EPS_ATTN).then_inc(S["VEQ"], 1)
                p.emit("vector", v_sp)
        veq_after[i] = p.cnt("VEQ")

        for tt in range(4):
            for half in range(2):
                bid = 2 + half
                w_bf = bank[bid]["frees"]
                my_fill = p.add("MMF", 1)

                def t_v(t, tt=tt, half=half, bid=bid, w_bf=w_bf, xTb=xTb, i=i):
                    t.wait_ge(S[f"BF{bid}"], w_bf)
                    t.wait_ge(S[f"LXT{i % 2}"], lxt_after[i])
                    for kk in range(8):
                        inst = t.matmul(B[bid][:, :],
                                        xTb[:, kk * ST + tt * 128: kk * ST + (tt + 1) * 128],
                                        wT["v"][:, kk * D + half * 512: kk * D + (half + 1) * 512],
                                        start=(kk == 0), stop=(kk == 7))
                    inst.then_inc(S["MMF"], 1)
                p.emit("tensor", t_v)

                def a_v(s, my_fill=my_fill, bid=bid, tt=tt, half=half,
                        prev_nsc=prev_nsc):
                    s.wait_ge(S["MMF"], my_fill)
                    if tt == 0 and half == 0 and prev_nsc > 0:
                        s.wait_ge(S["NSC"], prev_nsc)
                    s.activation(out=vaug[:, tt, half * 8:(half + 1) * 8, 0:64],
                                 in_=B[bid][:, :], func=AF.Copy).then_inc(S[f"BF{bid}"], 1)
                p.emit("scalar", a_v)
                bank[bid]["frees"] += 1
        xt_cons[i] = p.cnt("MMF")
        vfree_after[i] = (bank[2]["frees"], bank[3]["frees"])

        def g_kf(g, i=i, t0=t0):
            g.wait_ge(S["VEQ"], veq_after[i])
            g.dma_start(out=kfT_h.rearrange("(ee p) t -> p ee t", p=128)[:, :, t0:t0 + ST],
                        in_=kfT.ap().rearrange("p (ee t) -> p ee t", ee=8)).then_inc(S["GKF"], 16)
        p.emit("gpsimd", g_kf)
        gkf_after = p.add("GKF", 16)
        gkf_prev[i] = gkf_after

        for c in range(4):
            cc = i * 4 + c

            def s_kf(sy, i=i, c=c, cc=cc, t0=t0, gkf_after=gkf_after):
                sy.wait_ge(S["GKF"], gkf_after)
                if cc >= 2:
                    sy.wait_ge(S["NSC"], cc - 1)
                sy.dma_start(out=kf[c % 2][:, :],
                             in_=kfT_h[:, t0 + c * 128: t0 + (c + 1) * 128],
                             transpose=True).then_inc(S[f"LKF{c % 2}"], 16)
            p.emit("sync", s_kf)
            lkf_after[cc] = p.add(f"LKF{c % 2}", 16)

    def stage_fC(i):
        b, stb = divmod(i, STB)
        t0 = i * ST

        def core_at(c):
            cc = i * 4 + c
            amb = amT[c % 2]

            for hc in range(2):
                my_ca = p.add("CA", 1)
                mev_pre = p.cnt("MEV")

                def t_at(t, cc=cc, c=c, hc=hc, i=i, mev_pre=mev_pre):
                    t.wait_ge(S["VEQ"], veq_after[i])
                    for j, bk in enumerate((4, 5)):
                        if hc == 1:
                            t.wait_ge(S["MEV"], mev_pre - 1 + j)
                        elif cc >= 1:
                            t.wait_ge(S["MEV"], mev_after[cc - 1] - 1 + j)
                        for pp in range(4):
                            h_even = hc * 8 + 2 * pp
                            ee = h_even // 2
                            csl = slice(ee * ST + c * 128, ee * ST + (c + 1) * 128)
                            hp = slice(0, 64) if j == 0 else slice(64, 128)
                            inst = t.matmul(B[bk][:, pp * 128:(pp + 1) * 128],
                                            kfT[hp, csl], qfT[hp, csl],
                                            start=True, stop=True,
                                            tile_position=(j * 64, 0))
                    inst.then_inc(S["CA"], 1)
                p.emit("tensor", t_at)

                def v_me(v, my_ca=my_ca, hc=hc, cc=cc, amb=amb):
                    v.wait_ge(S["CA"], my_ca)
                    if cc >= 2:
                        v.wait_ge(S["NSC"], nsc_after[cc - 2])
                    for j, bk in enumerate((4, 5)):
                        base = hc * 1024 + j * 512
                        v.tensor_mul(amb[:, base:base + 512], B[bk][:, :],
                                     mask4[:, :]).then_inc(S["MEV"], 1)
                p.emit("vector", v_me)
                p.add("MEV", 2)
                if hc == 0:
                    mev_mid_t[cc] = p.cnt("MEV")
            mev_after[cc] = p.cnt("MEV")

        def core_ns(c):
            cc = i * 4 + c
            cb = stb * 4 + c
            first = (cb == 0)
            amb = amT[c % 2]

            my_nsc = p.add("NSC", 1)
            wbk = {bb: bank[bb]["frees"] for bb in (2, 3, 6, 7)}
            mpair = (2, 3) if c % 2 == 0 else (6, 7)
            spair = (6, 7) if c % 2 == 0 else (2, 3)

            def t_ns(t, cc=cc, c=c, i=i, first=first, cb=cb, wbk=dict(wbk),
                     mpair=mpair, spair=spair, amb=amb):
                t.wait_ge(S["MEV"], mev_mid_t[cc])
                for bb in mpair:
                    t.wait_ge(S[f"BF{bb}"], wbk[bb])
                t.wait_ge(S[f"LKF{c % 2}"], lkf_after[cc])
                if cc >= 1:
                    t.wait_ge(S["KVE"], kve_after[cc - 1])
                kfc = kf[c % 2]
                for h in list(range(14)) + [14, 15]:
                    ee = h // 2
                    odd = h % 2
                    if h == 8:
                        t.wait_ge(S["MEV"], mev_after[cc])
                    if h >= 14:
                        nb, off = spair[odd], 447
                        if h == 14:
                            for bb in spair:
                                t.wait_ge(S[f"BF{bb}"], wbk[bb])
                    else:
                        nb, off = mpair[odd], (h // 2) * 65
                    pr = h // 2
                    kvb = 0 if pr < 4 else 1
                    kvcol = (pr % 4) * 65
                    kvs = kvcol + 260 * kvb
                    kvpart = slice(0, 64) if odd == 0 else slice(64, 128)
                    tile_col = (0, 0) if odd == 0 else (0, 64)
                    tile_row = (0, 0) if odd == 0 else (64, 0)
                    qsl = slice(ee * ST + c * 128, ee * ST + (c + 1) * 128)
                    hpart = slice(0, 64) if odd == 0 else slice(64, 128)
                    ams = slice(am_off(h), am_off(h) + 128)
                    vsl = slice(c * 1040 + h * 65, c * 1040 + (h + 1) * 65)
                    opener = h in (0, 1, 14, 15)
                    closer = h in (12, 13, 14, 15)
                    if not first:
                        t.matmul(B[nb][:, off:off + 65], qfT[hpart, qsl],
                                 kv_sb[kvpart, kvs:kvs + 65],
                                 start=opener, stop=False, tile_position=tile_row)
                    t.matmul(B[nb][:, off:off + 65], amb[:, ams], vbuf[:, vsl],
                             start=(first and opener), stop=closer)
                    kv_open = first and h in (0, 1, 8, 9)
                    kv_close = (cb == CCH - 1) and h in (6, 7, 14, 15)
                    t.matmul(B[kvb][kvpart, kvcol:kvcol + 65], kfc[:, h * 64:(h + 1) * 64],
                             vbuf[:, vsl],
                             start=kv_open, stop=kv_close, tile_position=tile_col,
                             skip_group_check=True)
                inst = t.matmul(B[4][:, 300:428], amb[:, 0:128], kf[c % 2][:, 0:128],
                                start=True, stop=True)
                inst.then_inc(S["NSC"], 1)
            p.emit("tensor", t_ns)
            nsc_after[cc] = my_nsc

            my_rcp = p.add("RCP", 1)

            def v_rc(v, cc=cc, mpair=mpair, spair=spair):
                v.wait_ge(S["NSC"], nsc_after[cc])
                v.tensor_scalar_add(
                    dtmp[:, 0:7],
                    B[mpair[0]][:, 0:455].rearrange("p (n v) -> p n v", v=65)[:, :, 64:65],
                    EPS_ATTN)
                v.tensor_scalar_add(
                    dtmp[:, 7:14],
                    B[mpair[1]][:, 0:455].rearrange("p (n v) -> p n v", v=65)[:, :, 64:65],
                    EPS_ATTN)
                v.tensor_scalar_add(dtmp[:, 14:15], B[spair[0]][:, 511:512], EPS_ATTN)
                v.tensor_scalar_add(dtmp[:, 15:16], B[spair[1]][:, 511:512], EPS_ATTN)
                v.reciprocal(recip[:, 0:8], dtmp[:, 0:8])
                v.reciprocal(recip[:, 8:16], dtmp[:, 8:16]).then_inc(S["RCP"], 1)
            p.emit("vector", v_rc)
            rcp_after[cc] = my_rcp

            if debug and cc < 4:
                def g_dbg(g, cc=cc):
                    g.wait_ge(S["RCP"], rcp_after[cc])
                    g.dma_start(out=dbg_h[:, cc * 16:(cc + 1) * 16], in_=recip[:, :]).then_inc(S["GDB"], 16)
                p.emit("gpsimd", g_dbg)
                p.add("GDB", 16)
                gdb_now = p.cnt("GDB")

                def v_rc_gate(v, gdb_now=gdb_now):
                    v.wait_ge(S["GDB"], gdb_now)
                p.emit("vector", v_rc_gate)

            my_div = p.add("DIV", 1)
            prev_gao = gao_after.get(i - 1, 0)

            def a_dv(s, cc=cc, c=c, prev_gao=prev_gao, mpair=mpair, spair=spair):
                s.wait_ge(S["NSC"], nsc_after[cc])
                s.wait_ge(S["RCP"], rcp_after[cc])
                if c == 0 and prev_gao > 0:
                    s.wait_ge(S["GAO"], prev_gao)
                for h in (14, 15, 12, 13, 0, 1, 2, 3, 4, 5, 6, 7, 8, 9, 10, 11):
                    if h >= 14:
                        nb, off, rcol = spair[h % 2], 447, h
                    else:
                        nb = mpair[h % 2]
                        off = (h // 2) * 65
                        rcol = (h // 2) + 7 * (h % 2)
                    inst = s.activation(out=aob[:, c * D + h * 64: c * D + (h + 1) * 64],
                                        in_=B[nb][:, off:off + 64], func=AF.Copy,
                                        scale=recip[:, rcol:rcol + 1])
                    if h >= 14:
                        inst.then_inc(S[f"BF{spair[h % 2]}"], 1)
                inst.then_inc(S["DIV"], 1)
            p.emit("scalar", a_dv)
            div_after[cc] = my_div
            for bb in (2, 3, 6, 7):
                bank[bb]["frees"] += 1

            def a_dvf(s, mpair=mpair):
                for j, bb in enumerate(mpair):
                    s.activation(out=jk2[:, j:j + 1], in_=B[bb][:, 447:448],
                                 func=AF.Copy).then_inc(S[f"BF{bb}"], 1)
            p.emit("scalar", a_dvf)

            my_kve = p.add("KVE", 1)

            def v_kv(v, cc=cc):
                v.wait_ge(S["NSC"], nsc_after[cc])
                v.tensor_copy(out=kv_sb[:, 0:260], in_=B[0][:, 0:260])
                v.tensor_copy(out=kv_sb[:, 260:520], in_=B[1][:, 0:260]).then_inc(S["KVE"], 1)
            p.emit("vector", v_kv)
            kve_after[cc] = my_kve

            if debug and cc == 0:
                def g_dbg2(g):
                    g.wait_ge(S["KVE"], 1)
                    g.dma_start(out=dbg_h[:, 64:584], in_=kv_sb[:, :]).then_inc(S["GDB"], 16)
                    g.wait_ge(S["MEV"], 4)
                    g.dma_start(out=dbg_h[:, 584:584 + 2048], in_=amT[0][:, :]).then_inc(S["GDB"], 16)
                p.emit("gpsimd", g_dbg2)
                p.add("GDB", 32)
                gdb_now2 = p.cnt("GDB")

                def v_kv_gate(v, gdb_now2=gdb_now2):
                    v.wait_ge(S["GDB"], gdb_now2)
                p.emit("vector", v_kv_gate)

        for c in range(4):
            core_at(c)
            core_ns(c)

        def g_ao(g, i=i, t0=t0):
            g.wait_ge(S["DIV"], div_after[i * 4 + 3])
            for tt in range(4):
                g.dma_start(out=ao_h[t0 + tt * 128: t0 + (tt + 1) * 128, :],
                            in_=aob[:, tt * D:(tt + 1) * D]).then_inc(S["GAO"], 16)
        p.emit("gpsimd", g_ao)
        gao_after[i] = p.add("GAO", 16 * 4)

        if 2 <= i + 1 < NST:
            stage_sXT(i + 1)

    def stage_bW(i):
        t0 = i * ST

        for tt in range(4):
            tile = i * 4 + tt
            tb = tt % 2

            def s_xr(g, i=i, t0=t0, tt=tt, tb=tb, tile=tile):
                if tile >= 2:
                    g.wait_ge(S["VLN"], xres_free[tile - 2])
                g.dma_start(out=xres[tb][:, :],
                            in_=x_in[t0 + tt * 128: t0 + (tt + 1) * 128, :]).then_inc(S[f"LXR{tb}"], 16)
            p.emit("gpsimd", s_xr)
            lxr_after[tile] = p.add(f"LXR{tb}", 16)

            fills = []
            for half in range(2):
                bid = 2 + half
                w_bf = bank[bid]["frees"]
                my_fill = p.add("MMF", 1)
                fills.append(my_fill)

                def t_wo(t, tt=tt, half=half, bid=bid, w_bf=w_bf, i=i):
                    t.wait_ge(S[f"BF{bid}"], w_bf)
                    t.wait_ge(S["LAO"], lao_after[i])
                    for kk in range(8):
                        inst = t.matmul(B[bid][:, :],
                                        aoT[:, kk * ST + tt * 128: kk * ST + (tt + 1) * 128],
                                        wT["o"][:, kk * D + half * 512: kk * D + (half + 1) * 512],
                                        start=(kk == 0), stop=(kk == 7))
                    inst.then_inc(S["MMF"], 1)
                p.emit("tensor", t_wo)

            my_vln = p.add("VLN", 1)
            prev_aln = aln_lnexp.get(tile - 2, 0)

            def v_u(v, fills=tuple(fills), tb=tb, tile=tile, prev_aln=prev_aln):
                v.wait_ge(S["MMF"], fills[1])
                v.wait_ge(S[f"LXR{tb}"], lxr_after[tile])
                if prev_aln > 0:
                    v.wait_ge(S["ALN"], prev_aln)
                v.tensor_add(ubuf[tb][:, 0:512], xres[tb][:, 0:512], B[2][:, :]).then_inc(S["BF2"], 1)
                v.tensor_add(ubuf[tb][:, 512:1024], xres[tb][:, 512:1024], B[3][:, :]).then_inc(S["BF3"], 1)
                v.bn_stats(stat[tb][:, 0:6], ubuf[tb][:, 0:512])
                v.bn_stats(stat[tb][:, 6:12], ubuf[tb][:, 512:1024])
                v.tensor_copy(out=vjk[:, :], in_=iotap[:, :])
                v.bn_aggr(mv[tb][:, :], stat[tb][:, :]).then_inc(S["VLN"], 1)
            p.emit("vector", v_u)
            bank[2]["frees"] += 1
            bank[3]["frees"] += 1
            xres_free[tile] = my_vln

            my_aln = p.add("ALN", 1)
            aln_lnexp[tile] = my_aln

            def a_ln(s, my_vln=my_vln, tb=tb):
                s.wait_ge(S["VLN"], my_vln)
                s.activation(out=lnt[tb][:, 0:1], in_=mv[tb][:, 1:2], func=AF.Ln, bias=cst_eps[:, :])
                s.activation(out=lnt[tb][:, 1:2], in_=lnt[tb][:, 0:1], func=AF.Exp, scale=cst_mh[:, :])\
                 .then_inc(S["ALN"], 1)
            p.emit("scalar", a_ln)

            my_vln2 = p.add("VLN", 1)
            u_free[tile] = my_vln2

            def v_x1(v, my_aln=my_aln, tb=tb, tile=tile):
                v.wait_ge(S["ALN"], my_aln)
                if tile >= 2:
                    v.wait_ge(S["GX1"], x1_free.get(tile - 2, 0))
                v.tensor_scalar(x1b[tb][:, :], ubuf[tb][:, :], mv[tb][:, 0:1], lnt[tb][:, 1:2],
                                ALU.subtract, ALU.mult).then_inc(S["VLN"], 1)
            p.emit("vector", v_x1)

            my_aln2 = p.add("ALN", 1)

            def a_hi(s, my_vln2=my_vln2, tb=tb):
                s.wait_ge(S["VLN"], my_vln2)
                s.activation(out=x1hi[tb][:, :], in_=x1b[tb][:, :], func=AF.Copy)\
                 .then_inc(S["ALN"], 1)
            p.emit("scalar", a_hi)

            my_vln3 = p.add("VLN", 1)
            vlo_after[tile] = my_vln3

            def v_lo(v, my_aln2=my_aln2, tb=tb):
                v.wait_ge(S["ALN"], my_aln2)
                v.tensor_sub(x1lo[tb][:, :], x1b[tb][:, :], x1hi[tb][:, :]).then_inc(S["VLN"], 1)
            p.emit("vector", v_lo)

            gx1_pre = p.cnt("GX1")

            def g_x1(g, my_vln3=my_vln3, tb=tb, t0=t0, tt=tt, gx1_pre=gx1_pre):
                g.wait_ge(S["VLN"], my_vln3)
                g.wait_ge(S["GX1"], gx1_pre)
                r0 = t0 + tt * 128
                g.dma_start(out=x1_h[r0:r0 + 128, :], in_=x1b[tb][:, :]).then_inc(S["GX1"], 16)
                g.dma_start(out=x1hi_h[r0:r0 + 128, :], in_=x1hi[tb][:, :]).then_inc(S["GX1"], 16)
                g.dma_start(out=x1lo_h[r0:r0 + 128, :], in_=x1lo[tb][:, :]).then_inc(S["GX1"], 16)
            p.emit("gpsimd", g_x1)
            x1_free[tile] = p.add("GX1", 48)
        wo_cons[i] = p.cnt("MMF")
        gx1_after[i] = p.cnt("GX1")

    def stage_bL(i):
        for tt in range(4):
            col = i * 4 + tt
            bid = 2 + (tt % 2)
            w_bf = bank[bid]["frees"]
            my_fill = p.add("MMF", 1)

            def t_lg(t, tt=tt, i=i, w_bf=w_bf, bid=bid):
                t.wait_ge(S[f"BF{bid}"], w_bf)
                t.wait_ge(S["LHT"], lht_after[i])
                for kk in range(8):
                    tsl = slice(kk * ST + tt * 128, kk * ST + (tt + 1) * 128)
                    t.matmul(B[bid][:, 0:8], hiT[:, tsl], wg_hl[:, kk * 16:kk * 16 + 8],
                             start=(kk == 0), stop=False)
                    t.matmul(B[bid][:, 0:8], hiT[:, tsl], wg_hl[:, kk * 16 + 8:kk * 16 + 16],
                             start=False, stop=False)
                    inst = t.matmul(B[bid][:, 0:8], loT[:, tsl], wg_hl[:, kk * 16:kk * 16 + 8],
                                    start=False, stop=(kk == 7))
                inst = t.matmul(B[bid][:, 128:256], hiT[:, tsl], hiT[:, 0:128],
                                start=True, stop=True)
                inst.then_inc(S["MMF"], 1)
            p.emit("tensor", t_lg)

            my_vlg = p.add("VLG", 1)

            def v_lg(v, my_fill=my_fill, bid=bid):
                v.wait_ge(S["MMF"], my_fill)
                v.reduce_max(smt[:, 0:1], B[bid][:, 0:8], axis=AX.X, negate=True)
                v.reduce_max(smt[:, 2:3], B[bid][:, 0:8], axis=AX.X).then_inc(S["VLG"], 1)
            p.emit("vector", v_lg)

            my_aln3 = p.add("ALN", 1)

            def a_sm(s, my_vlg=my_vlg, bid=bid):
                s.wait_ge(S["VLG"], my_vlg)
                s.activation(out=lge[:, :], in_=B[bid][:, 0:8], func=AF.Exp,
                             bias=smt[:, 0:1], accum_out=smt[:, 1:2]).then_inc(S["ALN"], 1)
            p.emit("scalar", a_sm)

            def v_gb(v, my_aln3=my_aln3, col=col, bid=bid):
                v.wait_ge(S["ALN"], my_aln3)
                v.reciprocal(gate_c[:, col:col + 1], smt[:, 1:2])
                v.tensor_single_scalar(ohb[:, :], B[bid][:, 0:8], smt[:, 2:3],
                                       ALU.is_equal).then_inc(S[f"BF{bid}"], 1)
                v.tensor_copy(out=vjk[:, :], in_=iotap[:, :])
                v.scalar_tensor_tensor(lge[:, :], ohb[:, :], 1.0, iota8r[:, :],
                                       ALU.mult, ALU.mult,
                                       accum_out=best_c[:, col:col + 1]).then_inc(S["VLG"], 1)
            p.emit("vector", v_gb)
            bank[bid]["frees"] += 1
            p.add("VLG", 1)
        lg_cons[i] = p.cnt("MMF")

    for i in range(NST):
        stage_fQ(i)
        if i >= 2:
            stage_bL(i - 2)
        stage_fC(i)
        if i >= 1:
            stage_bW(i - 1)
    stage_sAO(NST - 1)
    stage_sHT(NST - 2)
    stage_bW(NST - 1)
    stage_bL(NST - 2)
    stage_sHT(NST - 1)
    stage_bL(NST - 1)

    es1.close()
    trb = E(nc.sbuf_tensor("trb", [128, 128], BF16))
    brep = [E(nc.sbuf_tensor(f"brep{j}", [8, FCH], BF16)) for j in range(2)]
    bufM = E(nc.sbuf_tensor("bufM", [8, FCH], F32))
    bufS = E(nc.sbuf_tensor("bufS", [8, FCH], F32))
    bufC = [E(nc.sbuf_tensor(f"bufC{j}", [8, FCH], F32)) for j in range(2)]
    run8 = E(nc.sbuf_tensor("run8", [8, 1], F32))
    carry8 = E(nc.sbuf_tensor("carry8", [8, 1], F32))
    ag_sb = E(nc.sbuf_tensor("ag_sb", [8 * n_cores, 1], F32))
    cntf = [E(nc.sbuf_tensor(f"cntf{j}", [1, FCH], F32)) for j in range(2)]
    gatef = [E(nc.sbuf_tensor(f"gatef{j}", [1, FCH], BF16)) for j in range(2)]
    gf = [E(nc.sbuf_tensor(f"gf{j}", [1, FCH], F32)) for j in range(2)]
    g4 = [E(nc.sbuf_tensor(f"g4_{j}", [128, 4], F32)) for j in range(2)]
    we_sb = E(nc.sbuf_tensor("we_sb", [128, 8 * D], BF16))
    hiT2 = E(nc.sbuf_tensor("hiT2", [128, 8 * ST], BF16))
    hiP = [hiT, hiT2]
    lg4_st = {}

    def stage_p2load(i):
        t0 = i * ST
        hib = hiP[i % 2]

        def s_p2(sy, i=i, t0=t0, hib=hib):
            if i < 2:
                sy.wait_ge(S["MMF"], lg_cons[NST - 1])
            else:
                sy.wait_ge(S["MMF"], eo_cons[i - 2])
                sy.wait_ge(S["VP2"], vp2_st[i - 2])
            for kk in range(8):
                sy.dma_start(out=hib[:, kk * ST:(kk + 1) * ST],
                             in_=x1hi_h[t0:t0 + ST, kk * 128:(kk + 1) * 128],
                             transpose=True).then_inc(S["LP2"], 16)
        p.emit("sync", s_p2)
        lp2_st[i] = p.add("LP2", 16 * 8)

    def emit_s_we():
        def s_we(sy):
            sy.wait_ge(S["MMF"], lg_cons[NST - 1])
            sy.dma_start(out=we_sb.ap().rearrange("p (kk e) -> p kk e", kk=8),
                         in_=wbf_h["e"].rearrange("(kk p) e -> p kk e", p=128)).then_inc(S["LP2"], 16)
        p.emit("sync", s_we)
        return p.add("LP2", 16)

    vlg_all = p.cnt("VLG")

    def v_cols(v):
        v.wait_ge(S["VLG"], vlg_all)
        v.tensor_copy(out=vjk[:, :], in_=iotap[:, :])
        v.tensor_copy(out=gate_cb[:, 0:NTT], in_=gate_c[:, :])
        v.tensor_copy(out=best_cb[:, 0:NTT], in_=best_c[:, :]).then_inc(S["VTL"], 1)
    p.emit("vector", v_cols)
    p.add("VTL", 1)

    def g_cols(g):
        g.wait_ge(S["VTL"], 1)
        g.memset(run8[:, :], 0.0)
        g.dma_start(out=bcol_h[:, :], in_=best_cb[:, :]).then_inc(S["GTL"], 16)
        g.dma_start(out=gcol_h[:, :], in_=gate_cb[:, :]).then_inc(S["GTL"], 16)
    p.emit("gpsimd", g_cols)
    p.add("GTL", 32)

    def s_tr(sy):
        sy.wait_ge(S["GTL"], 32)
        sy.dma_start(out=trb[:, :], in_=bcol_h[:, :], transpose=True).then_inc(S["LTL"], 16)
    p.emit("sync", s_tr)
    p.add("LTL", 16)

    def g_tok(g):
        g.wait_ge(S["LTL"], 16)
        g.dma_start(out=btok_h[:, :], in_=trb[:, :]).then_inc(S["GTL"], 16)
    p.emit("gpsimd", g_tok)
    p.add("GTL", 16)

    def s_tr2(sy):
        sy.wait_ge(S["GTL"], 48)
        sy.dma_start(out=trb[:, :], in_=gcol_h[:, :], transpose=True).then_inc(S["LTL"], 16)
    p.emit("sync", s_tr2)
    p.add("LTL", 16)

    def g_tok2(g):
        g.wait_ge(S["LTL"], 32)
        g.dma_start(out=gtok_h[:, :], in_=trb[:, :]).then_inc(S["GTL"], 16)
    p.emit("gpsimd", g_tok2)
    p.add("GTL", 16)
    gtl_tok = p.cnt("GTL")

    btok_flat = btok_h.rearrange("a b -> (a b)")
    gtok_flat = gtok_h.rearrange("a b -> (a b)")

    stage_p2load(0)

    vtl_eq1 = {}
    for k in range(NFC):
        w_vtl = vtl_eq1.get(k - 2, 0)

        def s_br(sy, k=k, w_vtl=w_vtl):
            sy.wait_ge(S["GTL"], gtl_tok)
            if w_vtl:
                sy.wait_ge(S["VTL"], w_vtl)
            sy.dma_start(out=brep[k % 2][:, :],
                         in_=btok_flat[k * FCH:(k + 1) * FCH]
                         .rearrange("(a t) -> a t", a=1).partition_broadcast(8)).then_inc(S["LTL"], 16)
        p.emit("sync", s_br)
        my_ltl = p.add("LTL", 16)

        def v_scan(v, my_ltl=my_ltl, k=k):
            v.wait_ge(S["LTL"], my_ltl)
            v.wait_ge(S["GPRE"], 1)
            v.tensor_single_scalar(bufM[:, :], brep[k % 2][:, :], iota8[:, :],
                                   ALU.is_equal).then_inc(S["VTL"], 1)
            v.tensor_copy(out=vjk[:, :], in_=iotap[:, :])
            v.tensor_tensor_scan(out=bufS[:, :], data0=bufM[:, :], data1=bufM[:, :],
                                 initial=run8[:, :], op0=ALU.add, op1=ALU.bypass)
            v.tensor_copy(out=vjk[:, :], in_=iotap[:, :])
            v.tensor_copy(out=run8[:, :], in_=bufS[:, FCH - 1:FCH]).then_inc(S["VTL"], 1)
        p.emit("vector", v_scan)
        vtl_eq1[k] = p.add("VTL", 1)
        p.add("VTL", 1)
    vtl_sweep1 = p.cnt("VTL")

    gtl_ccin = p.cnt("GTL") + 16

    def g_cc(g):
        g.wait_ge(S["VTL"], vtl_sweep1)
        g.dma_start(out=cc_in[:, :], in_=run8[:, :]).then_inc(S["GTL"], 16)
        g.wait_ge(S["GTL"], gtl_ccin)
        g.collective_compute("AllGather", ALU.bypass, ins=[cc_in[:, :]], outs=[cc_out[:, :]],
                             replica_groups=[list(range(n_cores))]).then_inc(S["CCD"], 1)
    p.emit("gpsimd", g_cc)
    p.add("GTL", 16)
    p.add("CCD", 1)

    def s_ag(sy):
        sy.wait_ge(S["CCD"], 1)
        sy.dma_start(out=ag_sb[:, :], in_=cc_out[:, :]).then_inc(S["LTL"], 16)
    p.emit("sync", s_ag)
    ltl_ag = p.add("LTL", 16)

    lp2_we = emit_s_we()
    if NST > 1:
        stage_p2load(1)

    carry_fill = p.add("MMF", 1)

    def t_carry(t):
        t.wait_ge(S["LTL"], ltl_ag)
        t.matmul(B[4][0:8, 0:1], sel_sb[:, :], ag_sb[:, :], start=True, stop=True)
        t.matmul(B[4][0:1, 64:192], ones8[:, :], bufM[:, 0:128],
                 start=True, stop=True).then_inc(S["MMF"], 1)
    p.emit("tensor", t_carry)

    my_vtl = p.add("VTL", 1)

    def v_carry(v):
        v.wait_ge(S["MMF"], carry_fill)
        v.tensor_copy(out=carry8[:, :], in_=B[4][0:8, 0:1]).then_inc(S["BF4"], 1)
        v.tensor_copy(out=vjk[:, :], in_=iotap[:, :])
        v.tensor_copy(out=run8[:, :], in_=carry8[:, :]).then_inc(S["VTL"], 1)
    p.emit("vector", v_carry)

    vtl_eq2 = {}
    vg_vtl = {}
    sumfill = {}
    gg_gp2 = {}
    for k in range(NFC):
        w_vtl = vtl_eq2.get(k - 2, 0)
        w_vg = vg_vtl.get(k - 2, 0)

        def s_br2(sy, k=k, w_vtl=w_vtl, w_vg=w_vg):
            sy.wait_ge(S["VTL"], max(w_vtl, w_vg, vtl_eq1[NFC - 1] if k < 2 else 0))
            sy.dma_start(out=brep[k % 2][:, :],
                         in_=btok_flat[k * FCH:(k + 1) * FCH]
                         .rearrange("(a t) -> a t", a=1).partition_broadcast(8)).then_inc(S["LTL"], 16)
            sy.dma_start(out=gatef[k % 2][:, :],
                         in_=gtok_flat[k * FCH:(k + 1) * FCH]
                         .rearrange("(a t) -> a t", a=1)).then_inc(S["LTL"], 16)
        p.emit("sync", s_br2)
        my_ltl = p.add("LTL", 32)

        def v_scan2(v, my_ltl=my_ltl, k=k, w_sum=sumfill.get(k - 2, 0)):
            v.wait_ge(S["LTL"], my_ltl)
            v.tensor_single_scalar(bufM[:, :], brep[k % 2][:, :], iota8[:, :],
                                   ALU.is_equal).then_inc(S["VTL"], 1)
            v.tensor_copy(out=vjk[:, :], in_=iotap[:, :])
            v.tensor_tensor_scan(out=bufS[:, :], data0=bufM[:, :], data1=bufM[:, :],
                                 initial=run8[:, :], op0=ALU.add, op1=ALU.bypass)
            v.tensor_copy(out=vjk[:, :], in_=iotap[:, :])
            v.tensor_copy(out=run8[:, :], in_=bufS[:, FCH - 1:FCH]).then_inc(S["VTL"], 1)
            if w_sum:
                v.wait_ge(S["MMF"], w_sum)
            v.tensor_mul(bufC[k % 2][:, :], bufM[:, :], bufS[:, :]).then_inc(S["VTL"], 1)
        p.emit("vector", v_scan2)
        p.add("VTL", 3)
        vtl_eq2[k] = p.cnt("VTL") - 2
        my_vtl2 = p.cnt("VTL")

        for j in range(FCH // 512):
            bk = 4 + j
            my_fill = p.add("MMF", 1)

            def t_sum(t, j=j, k=k, bk=bk, my_vtl2=my_vtl2):
                t.wait_ge(S[f"BF{bk}"], (1 if bk == 4 else 0) + max(0, k))
                t.wait_ge(S["VTL"], my_vtl2)
                t.matmul(B[bk][0:1, :], ones8[:, :], bufC[k % 2][:, j * 512:(j + 1) * 512],
                         start=True, stop=True).then_inc(S["MMF"], 1)
            p.emit("tensor", t_sum)

            def v_cf(v, my_fill=my_fill, j=j, k=k, bk=bk):
                v.wait_ge(S["MMF"], my_fill)
                v.tensor_copy(out=cntf[k % 2][:, j * 512:(j + 1) * 512],
                              in_=B[bk][0:1, :]).then_inc(S[f"BF{bk}"], 1)
            p.emit("vector", v_cf)
        sumfill[k] = p.cnt("MMF")

        my_vtl3 = p.add("VTL", 1)

        def v_g(v, k=k, w_gp2=gg_gp2.get(k - 2, 0)):
            if w_gp2:
                v.wait_ge(S["GP2"], w_gp2)
            v.tensor_copy(out=vjk[:, :], in_=iotap[:, :])
            v.tensor_single_scalar(bufM[0:1, :], cntf[k % 2][:, :], float(CAP) + 0.49, ALU.is_le)
            v.tensor_copy(out=vjk[:, :], in_=iotap[:, :])
            v.tensor_mul(gf[k % 2][:, :], bufM[0:1, :], gatef[k % 2][:, :]).then_inc(S["VTL"], 1)
        p.emit("vector", v_g)
        vg_vtl[k] = my_vtl3

        def g_g(g, k=k, my_vtl3=my_vtl3):
            g.wait_ge(S["VTL"], my_vtl3)
            g.dma_start(out=g_h[0:1, k * FCH:(k + 1) * FCH], in_=gf[k % 2][:, :]).then_inc(S["GP2"], 16)
        p.emit("gpsimd", g_g)
        gg_gp2[k] = p.add("GP2", 16)
    gp2_all = p.cnt("GP2")

    for i in range(NST):
        t0 = i * ST
        hib = hiP[i % 2]

        def s_g4(g, i=i, t0=t0):
            g.wait_ge(S["GP2"], 16 * (i // 2 + 1))
            if i >= 2:
                g.wait_ge(S["VP2"], vp2_st[i - 2])
            g.dma_start(out=g4[i % 2][:, :],
                        in_=g_h[0, t0:t0 + ST].rearrange("(tt p) -> p tt", p=128),
                        allow_slow_non_contiguous=True).then_inc(S["LG4"], 16)
        p.emit("gpsimd", s_g4)
        lg4_st[i] = p.add("LG4", 16)

        for tt in range(4):
            tile = i * 4 + tt
            tb = tt % 2
            pr = (2, 3) if tile % 2 == 0 else (6, 7)

            def s_x1r(g, i=i, t0=t0, tt=tt, tb=tb, tile=tile):
                if tile >= 2:
                    g.wait_ge(S["VP2"], p2_tile_free[tile - 2])
                g.dma_start(out=xres[tb][:, :],
                            in_=x1_h[t0 + tt * 128: t0 + (tt + 1) * 128, :]).then_inc(S[f"LXR{tb}"], 16)
            p.emit("gpsimd", s_x1r)
            lp2_x1[tile] = p.add(f"LXR{tb}", 16)

            fills = []
            for half in range(2):
                bid = pr[half]
                w_bf = bank[bid]["frees"]
                my_fill = p.add("MMF", 1)
                fills.append(my_fill)

                def t_eo(t, tt=tt, half=half, bid=bid, w_bf=w_bf, i=i, hib=hib):
                    t.wait_ge(S[f"BF{bid}"], w_bf)
                    t.wait_ge(S["LP2"], max(lp2_st[i], lp2_we))
                    for kk in range(8):
                        t.matmul(B[bid][:, :],
                                 hib[:, kk * ST + tt * 128: kk * ST + (tt + 1) * 128],
                                 we_sb[:, kk * D + half * 512: kk * D + (half + 1) * 512],
                                 start=(kk == 0), stop=False)
                    inst = t.matmul(B[bid][:, :], ones1b[:, :],
                                    be_b[0:1, half * 512:(half + 1) * 512],
                                    start=False, stop=True)
                    inst.then_inc(S["MMF"], 1)
                p.emit("tensor", t_eo)

            my_vp2 = p.add("VP2", 1)
            prev_ap2 = ap2_lnexp.get(tile - 2, 0)

            def v_u2(v, fills=tuple(fills), tb=tb, tile=tile, tt=tt, i=i,
                     prev_ap2=prev_ap2, pr=pr):
                v.wait_ge(S["MMF"], fills[1])
                v.wait_ge(S["LG4"], lg4_st[i])
                if prev_ap2 > 0:
                    v.wait_ge(S["AP2"], prev_ap2)
                v.wait_ge(S[f"LXR{tb}"], lp2_x1[tile])
                v.scalar_tensor_tensor(ubuf[tb][:, 0:512], B[pr[0]][:, :], g4[i % 2][:, tt:tt + 1],
                                       xres[tb][:, 0:512], ALU.mult, ALU.add).then_inc(S[f"BF{pr[0]}"], 1)
                v.scalar_tensor_tensor(ubuf[tb][:, 512:1024], B[pr[1]][:, :], g4[i % 2][:, tt:tt + 1],
                                       xres[tb][:, 512:1024], ALU.mult, ALU.add).then_inc(S[f"BF{pr[1]}"], 1)
                v.bn_stats(stat[tb][:, 0:6], ubuf[tb][:, 0:512])
                v.bn_stats(stat[tb][:, 6:12], ubuf[tb][:, 512:1024])
                v.tensor_copy(out=vjk[:, :], in_=iotap[:, :])
                v.bn_aggr(mv[tb][:, :], stat[tb][:, :]).then_inc(S["VP2"], 1)
            p.emit("vector", v_u2)
            bank[pr[0]]["frees"] += 1
            bank[pr[1]]["frees"] += 1

            my_ap2 = p.add("AP2", 1)
            ap2_lnexp[tile] = my_ap2

            def a_ln2(s, my_vp2=my_vp2, tb=tb):
                s.wait_ge(S["VP2"], my_vp2)
                s.activation(out=lnt[tb][:, 0:1], in_=mv[tb][:, 1:2], func=AF.Ln, bias=cst_eps[:, :])
                s.activation(out=lnt[tb][:, 1:2], in_=lnt[tb][:, 0:1], func=AF.Exp, scale=cst_mh[:, :])\
                 .then_inc(S["AP2"], 1)
            p.emit("scalar", a_ln2)

            my_vp22 = p.add("VP2", 1)
            p2_tile_free[tile] = my_vp22

            def v_y(v, my_ap2=my_ap2, tb=tb, tile=tile):
                v.wait_ge(S["AP2"], my_ap2)
                if tile >= 2:
                    v.wait_ge(S["GY"], gy_after.get(tile - 2, 0))
                v.tensor_scalar(x1b[tb][:, :], ubuf[tb][:, :], mv[tb][:, 0:1], lnt[tb][:, 1:2],
                                ALU.subtract, ALU.mult).then_inc(S["VP2"], 1)
            p.emit("vector", v_y)

            gy_pre = p.cnt("GY")

            def g_y(g, my_vp22=my_vp22, tb=tb, t0=t0, tt=tt, gy_pre=gy_pre):
                g.wait_ge(S["VP2"], my_vp22)
                g.wait_ge(S["GY"], gy_pre)
                g.dma_start(out=y_out[t0 + tt * 128: t0 + (tt + 1) * 128, :],
                            in_=x1b[tb][:, :]).then_inc(S["GY"], 16)
            p.emit("gpsimd", g_y)
            gy_after[tile] = p.add("GY", 16)
        vp2_st[i] = p.cnt("VP2")
        eo_cons[i] = p.cnt("MMF")
        if i + 2 < NST:
            stage_p2load(i + 2)

    gy_all = p.cnt("GY")

    def s_fin(sy):
        sy.wait_ge(S["GY"], gy_all)
    p.emit("sync", s_fin)

    for name in p.counts:
        S[name] = es.enter_context(nc.semaphore(name))
    with nc.Block() as block:
        for eng_name in Prog.ENGS:
            def mk(eng_name):
                def body(eng):
                    for fn in p.items[eng_name]:
                        fn(eng)
                return body
            getattr(block, eng_name)(mk(eng_name))
    es.close()
    return nc


def shard_inputs(inputs, n_cores, bpc):
    x = np.ascontiguousarray(np.asarray(inputs["L0_attn_x"], dtype=np.float32))
    Bfull = x.shape[0]
    x2 = x.reshape(Bfull * T, D)
    maps = []
    for c in range(n_cores):
        sel = np.zeros((8 * n_cores, 8), np.float32)
        for r in range(c):
            sel[r * 8:(r + 1) * 8] = np.eye(8, dtype=np.float32)
        maps.append({
            "x": np.ascontiguousarray(x2[c * bpc * T:(c + 1) * bpc * T]),
            "wq": np.ascontiguousarray(np.asarray(inputs["L0_attn_Wq"], np.float32)),
            "wk": np.ascontiguousarray(np.asarray(inputs["L0_attn_Wk"], np.float32)),
            "wv": np.ascontiguousarray(np.asarray(inputs["L0_attn_Wv"], np.float32)),
            "wo": np.ascontiguousarray(np.asarray(inputs["L0_attn_Wo"], np.float32)),
            "we": np.ascontiguousarray(np.asarray(inputs["L0_ffn_w_expert"], np.float32)),
            "wg": np.ascontiguousarray(np.asarray(inputs["L0_ffn_w_gate"], np.float32)),
            "be": np.ascontiguousarray(np.asarray(inputs["L0_ffn_b_expert"], np.float32).reshape(1, D)),
            "sel": sel,
        })
    return maps


N_CORES = 8
BPC = 4
_CACHE = {}


def kernel(**inputs):
    if "nc" not in _CACHE:
        nc = bass.Bass("TRN2", target_bir_lowering=False, debug=False,
                       detect_race_conditions=False)
        build_program(nc, BPC, N_CORES)
        _CACHE["nc"] = nc
    nc = _CACHE["nc"]
    from concourse.bass_utils import run_bass_kernel_spmd
    maps = shard_inputs(inputs, N_CORES, BPC)
    res = run_bass_kernel_spmd(nc, maps, core_ids=list(range(N_CORES)))
    y = np.concatenate([np.asarray(res.results[c]["y"]) for c in range(N_CORES)], axis=0)
    return y.reshape(32, T, D)


# version 15
# speedup vs baseline: 1.2515x; 1.2515x over previous
import contextlib
import numpy as np
import concourse.bass as bass
import concourse.mybir as mybir

F32 = mybir.dt.float32
BF16 = mybir.dt.bfloat16
AF = mybir.ActivationFunctionType
ALU = mybir.AluOpType
AX = mybir.AxisListType

D = 1024
T = 2048
ST = 512
EPS_ATTN = 1e-6
EPS_LN = 1e-5
CAP = 100
FCH = 1024
RELAX_NS = 1


def am_off(h):
    hc = h // 8
    slot = (h % 8) // 2
    return hc * 1024 + (h % 2) * 512 + slot * 128


class Prog:
    ENGS = ("sync", "gpsimd", "tensor", "vector", "scalar")

    def __init__(self):
        self.items = {e: [] for e in self.ENGS}
        self.counts = {}

    def sem(self, name):
        self.counts.setdefault(name, 0)
        return name

    def add(self, name, n=1):
        self.counts[name] = self.counts.get(name, 0) + n
        return self.counts[name]

    def cnt(self, name):
        return self.counts.get(name, 0)

    def emit(self, eng, fn):
        self.items[eng].append(fn)


def build_program(nc, BPC, n_cores, debug=False):
    NT = BPC * T
    NST = NT // ST
    NTT = NT // 128
    STB = T // ST
    CCH = T // 128
    NFC = max(1, NT // FCH)

    x_in = nc.dram_tensor("x", [NT, D], F32, kind="ExternalInput").ap()
    w_in = {k: nc.dram_tensor(f"w{k}", [D, D], F32, kind="ExternalInput").ap()
            for k in ("q", "k", "v", "o", "e")}
    wg_in = nc.dram_tensor("wg", [D, 8], F32, kind="ExternalInput").ap()
    be_in = nc.dram_tensor("be", [1, D], F32, kind="ExternalInput").ap()
    sel_in = nc.dram_tensor("sel", [8 * n_cores, 8], F32, kind="ExternalInput").ap()
    y_out = nc.dram_tensor("y", [NT, D], F32, kind="ExternalOutput").ap()

    dbg = "ExternalOutput" if debug else "Internal"
    xbf_h = nc.dram_tensor("xbf_h", [NT, D], BF16, kind="Internal").ap()
    wbf_h = {k: nc.dram_tensor(f"w{k}_h", [D, D], BF16, kind="Internal").ap()
             for k in ("q", "k", "v", "o", "e")}
    kfT_h = nc.dram_tensor("kfT_h", [D, NT], BF16, kind=dbg).ap()
    ao_h = nc.dram_tensor("ao_h", [NT, D], BF16, kind=dbg).ap()
    x1_h = nc.dram_tensor("x1_h", [NT, D], F32, kind=dbg).ap()
    x1hi_h = nc.dram_tensor("x1hi_h", [NT, D], BF16, kind="Internal").ap()
    x1lo_h = nc.dram_tensor("x1lo_h", [NT, D], BF16, kind="Internal").ap()
    bcol_h = nc.dram_tensor("bcol_h", [128, 128], BF16, kind=dbg).ap()
    gcol_h = nc.dram_tensor("gcol_h", [128, 128], BF16, kind=dbg).ap()
    btok_h = nc.dram_tensor("btok_h", [128, 128], BF16, kind=dbg).ap()
    gtok_h = nc.dram_tensor("gtok_h", [128, 128], BF16, kind="Internal").ap()
    g_h = nc.dram_tensor("g_h", [1, NT], F32, kind=dbg).ap()
    dbg_h = nc.dram_tensor("dbg_h", [128, 16 * 4 + 520 + 2048 + 128], F32, kind=dbg).ap() if debug else None
    cc_in = nc.dram_tensor("cc_in", [8, 1], F32, kind="Internal").ap()
    cc_out = nc.dram_tensor("cc_out", [8 * n_cores, 1], F32, kind="Internal").ap()

    es = contextlib.ExitStack()
    es1 = contextlib.ExitStack()
    E = es.enter_context
    E1 = es1.enter_context
    wT = {k: E(nc.sbuf_tensor(f"wT{k}", [128, 8 * D], BF16)) for k in ("q", "k", "v", "o")}
    wg_f = E(nc.sbuf_tensor("wg_f", [128, 8 * 8], F32))
    wg_hl = E(nc.sbuf_tensor("wg_hl", [128, 8 * 16], BF16))
    be_f = E(nc.sbuf_tensor("be_f", [1, D], F32))
    be_b = E(nc.sbuf_tensor("be_b", [1, D], BF16))
    sel_sb = E(nc.sbuf_tensor("sel_sb", [8 * n_cores, 8], F32))
    ones1b = E(nc.sbuf_tensor("ones1b", [1, 128], BF16))
    onescol = E(nc.sbuf_tensor("onescol", [128, 1], BF16))
    ones8 = E(nc.sbuf_tensor("ones8", [8, 1], F32))
    cst_eps = E(nc.sbuf_tensor("cst_eps", [128, 1], F32))
    cst_mh = E(nc.sbuf_tensor("cst_mh", [128, 1], F32))
    iota8 = E(nc.sbuf_tensor("iota8", [8, 1], F32))
    iota8r = E(nc.sbuf_tensor("iota8r", [128, 8], F32))
    iotap = E(nc.sbuf_tensor("iotap", [128, 1], F32))
    mask4 = E(nc.sbuf_tensor("mask4", [128, 512], BF16))
    recip = E(nc.sbuf_tensor("recip", [128, 16], F32))
    dtmp = E(nc.sbuf_tensor("dtmp", [128, 16], F32))
    jk2 = E(nc.sbuf_tensor("jk2", [128, 4], F32))
    vjk = E(nc.sbuf_tensor("vjk", [128, 1], F32))
    xres = [E(nc.sbuf_tensor(f"xres{j}", [128, D], F32)) for j in range(2)]
    ubuf = [E(nc.sbuf_tensor(f"ubuf{j}", [128, D], F32)) for j in range(2)]
    x1b = [E(nc.sbuf_tensor(f"x1b{j}", [128, D], F32)) for j in range(2)]
    x1hi = [E(nc.sbuf_tensor(f"x1hi{j}", [128, D], BF16)) for j in range(2)]
    x1lo = [E(nc.sbuf_tensor(f"x1lo{j}", [128, D], BF16)) for j in range(2)]
    stat = [E(nc.sbuf_tensor(f"stat{j}", [128, 12], F32)) for j in range(2)]
    mv = [E(nc.sbuf_tensor(f"mv{j}", [128, 2], F32)) for j in range(2)]
    lnt = [E(nc.sbuf_tensor(f"lnt{j}", [128, 4], F32)) for j in range(2)]
    hiT = E(nc.sbuf_tensor("hiT", [128, 8 * ST], BF16))
    loT = E(nc.sbuf_tensor("loT", [128, 8 * ST], BF16))
    lgb = E(nc.sbuf_tensor("lgb", [128, 8], F32))
    lge = E(nc.sbuf_tensor("lge", [128, 8], F32))
    ohb = E(nc.sbuf_tensor("ohb", [128, 8], F32))
    smt = E(nc.sbuf_tensor("smt", [128, 3], F32))
    gate_c = E(nc.sbuf_tensor("gate_c", [128, NTT], F32))
    best_c = E(nc.sbuf_tensor("best_c", [128, NTT], F32))
    gate_cb = E(nc.sbuf_tensor("gate_cb", [128, 128], BF16))
    best_cb = E(nc.sbuf_tensor("best_cb", [128, 128], BF16))
    xT = [E1(nc.sbuf_tensor(f"xT{j}", [128, 8 * ST], BF16)) for j in range(2)]
    qfT = E1(nc.sbuf_tensor("qfT", [128, 8 * ST], BF16))
    kfT = E1(nc.sbuf_tensor("kfT", [128, 8 * ST], BF16))
    vbuf = E1(nc.sbuf_tensor("vbuf", [128, 4 * 1040], BF16))
    vaug = vbuf.ap().rearrange("p (c h v) -> p c h v", h=16, v=65)
    kf = [E1(nc.sbuf_tensor(f"kf{j}", [128, D], BF16)) for j in range(2)]
    spE = E1(nc.sbuf_tensor("spE", [128, 1024], F32))
    spL = E1(nc.sbuf_tensor("spL", [128, 1024], F32))
    iotac = spE[:, 0:512]
    mask4f = spL[:, 0:512]
    amT = [E1(nc.sbuf_tensor(f"amT{j}", [128, 2048], BF16)) for j in range(2)]
    kv_sb = E1(nc.sbuf_tensor("kv_sb", [128, 520], BF16))
    aob = E1(nc.sbuf_tensor("aob", [128, 4 * D], BF16))
    aoT = E1(nc.sbuf_tensor("aoT", [128, 8 * ST], BF16))
    B = [E(nc.psum_tensor(f"pb{j}", [128, 512], F32)) for j in range(8)]

    p = Prog()
    S = {}

    for nm in ("GW", "LW", "GPRE", "PRE", "LXT0", "LXT1", "MMF", "BF2", "BF3",
               "BF4", "BF5", "BF6", "BF7", "AEX", "LG4",
               "VEQ", "GKF", "LKF0", "LKF1", "CA", "MEV", "NSC", "RCP", "DIV", "KVE",
               "GAO", "LAO", "LXR0", "LXR1", "VLN", "ALN", "GX1", "LHT", "VLG",
               "VTL", "GTL", "LTL", "CCD", "GP2", "LP2", "VP2", "AP2", "GY", "GDB", "DBV"):
        p.sem(nm)
    for b in range(BPC):
        p.sem(f"GWX{b}")

    bank = {2: {"frees": 0}, 3: {"frees": 0}, 6: {"frees": 0}, 7: {"frees": 0}}

    def g_pre(g):
        for k in ("q", "k", "v", "o", "e"):
            g.dma_start(out=wbf_h[k][:, :], in_=w_in[k][:, :]).then_inc(S["GW"], 16)
        for b in range(BPC):
            g.dma_start(out=xbf_h[b * T:(b + 1) * T, :],
                        in_=x_in[b * T:(b + 1) * T, :]).then_inc(S[f"GWX{b}"], 16)
        g.iota(iota8[:, :], pattern=[[0, 1]], channel_multiplier=1,
               allow_small_or_imprecise_dtypes=True)
        g.iota(iota8r[:, :], pattern=[[1, 8]], channel_multiplier=0,
               allow_small_or_imprecise_dtypes=True)
        g.iota(iotap[:, :], pattern=[[0, 1]], channel_multiplier=1,
               allow_small_or_imprecise_dtypes=True)
        g.iota(iotac, pattern=[[0, 4], [1, 128]], channel_multiplier=0,
               allow_small_or_imprecise_dtypes=True)
        g.memset(ones1b[:, :], 1.0)
        g.memset(onescol[:, :], 1.0)
        g.memset(vaug[:, :, :, 64:65], 1.0)
        g.memset(gate_cb[:, :], 0.0)
        g.memset(cst_eps[:, :], EPS_LN)
        g.memset(cst_mh[:, :], -0.5)
        g.memset(best_cb[:, :], 0.0)
        g.memset(ones8[:, :], 1.0).then_inc(S["GPRE"], 1)
    p.emit("gpsimd", g_pre)
    p.add("GW", 16 * 5)
    for b in range(BPC):
        p.add(f"GWX{b}", 16)
    p.add("GPRE", 1)

    def s_pre(sy):
        sy.wait_ge(S["GW"], 16 * 5)
        for k in ("q", "k", "v", "o"):
            for kk in range(8):
                sy.dma_start(out=wT[k][:, kk * D:(kk + 1) * D],
                             in_=wbf_h[k][:, kk * 128:(kk + 1) * 128],
                             transpose=True).then_inc(S["LW"], 16)
        sy.dma_start(out=wg_f.ap().rearrange("p (kk e) -> p kk e", kk=8),
                     in_=wg_in.rearrange("(kk p) e -> p kk e", p=128)).then_inc(S["LW"], 16)
        sy.dma_start(out=be_f[:, :], in_=be_in[:, :]).then_inc(S["LW"], 16)
        sy.dma_start(out=sel_sb[:, :], in_=sel_in[:, :]).then_inc(S["LW"], 16)
    p.emit("sync", s_pre)
    p.add("LW", 16 * 35)

    def v_pre(v):
        v.wait_ge(S["GPRE"], 1)
        v.wait_ge(S["LW"], 16 * 35)
        v.tensor_single_scalar(mask4f, iotac, iotap[:, :], ALU.is_ge)
        v.tensor_copy(out=vjk[:, :], in_=iotap[:, :])
        v.tensor_copy(out=mask4[:, :], in_=mask4f)
        for kk in range(8):
            v.tensor_copy(out=wg_hl[:, kk * 16:kk * 16 + 8], in_=wg_f[:, kk * 8:(kk + 1) * 8])
        v.tensor_copy(out=vjk[:, :], in_=iotap[:, :])
        for kk in range(8):
            v.tensor_sub(wg_hl[:, kk * 16 + 8:kk * 16 + 16], wg_f[:, kk * 8:(kk + 1) * 8],
                         wg_hl[:, kk * 16:kk * 16 + 8])
        v.tensor_copy(out=be_b[:, :], in_=be_f[:, :]).then_inc(S["PRE"], 1)
    p.emit("vector", v_pre)
    p.add("PRE", 1)

    lxt_after = {}
    xt_cons = {}
    veq_after = {}
    qk_n = [0]
    vfree_after = {}
    lkf_after = {}
    mev_after = {}
    nsc_after = {}
    rcp_after = {}
    div_after = {}
    mev_mid_t = {}
    kve_after = {}
    gao_after = {}
    lao_after = {}
    wo_cons = {}
    lxr_after = {}
    u_free = {}
    xres_free = {}
    x1_free = {}
    vlo_after = {}
    gx1_after = {}
    lht_after = {}
    lg_cons = {}
    lp2_st = {}
    lp2_x1 = {}
    eo_cons = {}
    gkf_prev = {}
    aln_lnexp = {}
    ap2_lnexp = {}
    p2_tile_free = {}
    vp2_st = {}
    gy_after = {}

    def stage_sHT(i):
        t0 = i * ST

        def s_ht(sy, i=i, t0=t0):
            sy.wait_ge(S["GX1"], gx1_after[i])
            if i >= 1:
                sy.wait_ge(S["MMF"], lg_cons[i - 1])
            for kk in range(8):
                sy.dma_start(out=hiT[:, kk * ST:(kk + 1) * ST],
                             in_=x1hi_h[t0:t0 + ST, kk * 128:(kk + 1) * 128],
                             transpose=True).then_inc(S["LHT"], 16)
                sy.dma_start(out=loT[:, kk * ST:(kk + 1) * ST],
                             in_=x1lo_h[t0:t0 + ST, kk * 128:(kk + 1) * 128],
                             transpose=True).then_inc(S["LHT"], 16)
        p.emit("sync", s_ht)
        lht_after[i] = p.add("LHT", 16 * 16)

    def stage_sXT(i):
        b = i // STB
        t0 = i * ST
        xTb = xT[i % 2]

        def s_xt(sy, i=i, b=b, t0=t0, xTb=xTb):
            sy.wait_ge(S[f"GWX{b}"], 16)
            if i >= 2:
                sy.wait_ge(S["MMF"], xt_cons[i - 2])
            for kk in range(8):
                sy.dma_start(out=xTb[:, kk * ST:(kk + 1) * ST],
                             in_=xbf_h[t0:t0 + ST, kk * 128:(kk + 1) * 128],
                             transpose=True).then_inc(S[f"LXT{i % 2}"], 16)
        p.emit("sync", s_xt)
        lxt_after[i] = p.add(f"LXT{i % 2}", 16 * 8)

    def stage_sAO(i):
        t0 = i * ST

        def s_ao(sy, i=i, t0=t0):
            sy.wait_ge(S["GAO"], gao_after[i])
            if i >= 1:
                sy.wait_ge(S["MMF"], wo_cons[i - 1])
            for kk in range(8):
                sy.dma_start(out=aoT[:, kk * ST:(kk + 1) * ST],
                             in_=ao_h[t0:t0 + ST, kk * 128:(kk + 1) * 128],
                             transpose=True).then_inc(S["LAO"], 16)
        p.emit("sync", s_ao)
        lao_after[i] = p.add("LAO", 16 * 8)

    def stage_fQ(i):
        b, stb = divmod(i, STB)
        t0 = i * ST
        xTb = xT[i % 2]

        if i <= 1:
            stage_sXT(i)
        if i >= 2:
            stage_sHT(i - 2)
        if i >= 1:
            stage_sAO(i - 1)

        prev_nsc = nsc_after.get(i * 4 - 1, 0)
        prev_gkf = gkf_prev.get(i - 1, 0)
        for (nm, dstT) in (("q", qfT), ("k", kfT)):
            for ee in range(8):
                bid = 2 + (ee % 2)
                bf = f"BF{bid}"
                w_bf = bank[bid]["frees"]
                my_fill = p.add("MMF", 1)

                def t_qk(t, nm=nm, ee=ee, bid=bid, w_bf=w_bf, xTb=xTb, i=i):
                    t.wait_ge(S[f"BF{bid}"], w_bf)
                    t.wait_ge(S[f"LXT{i % 2}"], lxt_after[i])
                    t.wait_ge(S["PRE"], 1)
                    for kk in range(8):
                        inst = t.matmul(B[bid][:, :],
                                        wT[nm][:, kk * D + ee * 128: kk * D + (ee + 1) * 128],
                                        xTb[:, kk * ST:(kk + 1) * ST],
                                        start=(kk == 0), stop=(kk == 7))
                    inst.then_inc(S["MMF"], 1)
                p.emit("tensor", t_qk)

                qk_n[0] += 1
                veq_wait = max(0, qk_n[0] - 2)
                sp0 = (qk_n[0] % 2) * 512
                my_aex = p.add("AEX", 1)

                def a_sp(s, my_fill=my_fill, bid=bid, sp0=sp0, veq_wait=veq_wait):
                    s.wait_ge(S["MMF"], my_fill)
                    if veq_wait > 0:
                        s.wait_ge(S["VEQ"], veq_wait)
                    s.activation(out=spE[:, sp0:sp0 + 512], in_=B[bid][:, :],
                                 func=AF.Exp).then_inc(S[f"BF{bid}"], 1)
                    s.activation(out=spL[:, sp0:sp0 + 512], in_=spE[:, sp0:sp0 + 512],
                                 func=AF.Ln, bias=1.0).then_inc(S["AEX"], 1)
                p.emit("scalar", a_sp)
                bank[bid]["frees"] += 1

                my_veq = p.add("VEQ", 1)

                def v_sp(v, my_aex=my_aex, dstT=dstT, ee=ee, sp0=sp0,
                         prev_nsc=prev_nsc, prev_gkf=prev_gkf, nm=nm):
                    v.wait_ge(S["AEX"], my_aex)
                    if ee == 0 and prev_nsc > 0:
                        v.wait_ge(S["NSC"], prev_nsc)
                        if nm == "k":
                            v.wait_ge(S["GKF"], prev_gkf)
                    v.tensor_scalar_add(dstT[:, ee * ST:(ee + 1) * ST],
                                        spL[:, sp0:sp0 + 512], EPS_ATTN).then_inc(S["VEQ"], 1)
                p.emit("vector", v_sp)
        veq_after[i] = p.cnt("VEQ")

        for tt in range(4):
            for half in range(2):
                bid = 2 + half
                w_bf = bank[bid]["frees"]
                my_fill = p.add("MMF", 1)

                def t_v(t, tt=tt, half=half, bid=bid, w_bf=w_bf, xTb=xTb, i=i):
                    t.wait_ge(S[f"BF{bid}"], w_bf)
                    t.wait_ge(S[f"LXT{i % 2}"], lxt_after[i])
                    for kk in range(8):
                        inst = t.matmul(B[bid][:, :],
                                        xTb[:, kk * ST + tt * 128: kk * ST + (tt + 1) * 128],
                                        wT["v"][:, kk * D + half * 512: kk * D + (half + 1) * 512],
                                        start=(kk == 0), stop=(kk == 7))
                    inst.then_inc(S["MMF"], 1)
                p.emit("tensor", t_v)

                def a_v(s, my_fill=my_fill, bid=bid, tt=tt, half=half,
                        prev_nsc=prev_nsc):
                    s.wait_ge(S["MMF"], my_fill)
                    if tt == 0 and half == 0 and prev_nsc > 0:
                        s.wait_ge(S["NSC"], prev_nsc)
                    s.activation(out=vaug[:, tt, half * 8:(half + 1) * 8, 0:64],
                                 in_=B[bid][:, :], func=AF.Copy).then_inc(S[f"BF{bid}"], 1)
                p.emit("scalar", a_v)
                bank[bid]["frees"] += 1
        xt_cons[i] = p.cnt("MMF")
        vfree_after[i] = (bank[2]["frees"], bank[3]["frees"])

        def g_kf(g, i=i, t0=t0):
            g.wait_ge(S["VEQ"], veq_after[i])
            g.dma_start(out=kfT_h.rearrange("(ee p) t -> p ee t", p=128)[:, :, t0:t0 + ST],
                        in_=kfT.ap().rearrange("p (ee t) -> p ee t", ee=8)).then_inc(S["GKF"], 16)
        p.emit("gpsimd", g_kf)
        gkf_after = p.add("GKF", 16)
        gkf_prev[i] = gkf_after

        for c in range(4):
            cc = i * 4 + c

            def s_kf(sy, i=i, c=c, cc=cc, t0=t0, gkf_after=gkf_after):
                sy.wait_ge(S["GKF"], gkf_after)
                if cc >= 2:
                    sy.wait_ge(S["NSC"], cc - 1)
                sy.dma_start(out=kf[c % 2][:, :],
                             in_=kfT_h[:, t0 + c * 128: t0 + (c + 1) * 128],
                             transpose=True).then_inc(S[f"LKF{c % 2}"], 16)
            p.emit("sync", s_kf)
            lkf_after[cc] = p.add(f"LKF{c % 2}", 16)

    def stage_fC(i):
        b, stb = divmod(i, STB)
        t0 = i * ST

        def core_at(c):
            cc = i * 4 + c
            amb = amT[c % 2]

            for hc in range(2):
                my_ca = p.add("CA", 1)
                mev_pre = p.cnt("MEV")

                def t_at(t, cc=cc, c=c, hc=hc, i=i, mev_pre=mev_pre):
                    t.wait_ge(S["VEQ"], veq_after[i])
                    for j, bk in enumerate((4, 5)):
                        if hc == 1:
                            t.wait_ge(S["MEV"], mev_pre - 1 + j)
                        elif cc >= 1:
                            t.wait_ge(S["MEV"], mev_after[cc - 1] - 1 + j)
                        for pp in range(4):
                            h_even = hc * 8 + 2 * pp
                            ee = h_even // 2
                            csl = slice(ee * ST + c * 128, ee * ST + (c + 1) * 128)
                            hp = slice(0, 64) if j == 0 else slice(64, 128)
                            inst = t.matmul(B[bk][:, pp * 128:(pp + 1) * 128],
                                            kfT[hp, csl], qfT[hp, csl],
                                            start=True, stop=True,
                                            tile_position=(j * 64, 0))
                    inst.then_inc(S["CA"], 1)
                p.emit("tensor", t_at)

                def v_me(v, my_ca=my_ca, hc=hc, cc=cc, amb=amb):
                    v.wait_ge(S["CA"], my_ca)
                    if cc >= 2:
                        v.wait_ge(S["NSC"], nsc_after[cc - 2])
                    for j, bk in enumerate((4, 5)):
                        base = hc * 1024 + j * 512
                        v.tensor_mul(amb[:, base:base + 512], B[bk][:, :],
                                     mask4[:, :]).then_inc(S["MEV"], 1)
                p.emit("vector", v_me)
                p.add("MEV", 2)
                if hc == 0:
                    mev_mid_t[cc] = p.cnt("MEV")
            mev_after[cc] = p.cnt("MEV")

        def core_ns(c):
            cc = i * 4 + c
            cb = stb * 4 + c
            first = (cb == 0)
            amb = amT[c % 2]

            my_nsc = p.add("NSC", 1)
            wbk = {bb: bank[bb]["frees"] for bb in (2, 3, 6, 7)}
            mpair = (2, 3) if c % 2 == 0 else (6, 7)
            spair = (6, 7) if c % 2 == 0 else (2, 3)

            def t_ns(t, cc=cc, c=c, i=i, first=first, cb=cb, wbk=dict(wbk),
                     mpair=mpair, spair=spair, amb=amb):
                t.wait_ge(S["MEV"], mev_mid_t[cc])
                for bb in mpair:
                    t.wait_ge(S[f"BF{bb}"], wbk[bb])
                t.wait_ge(S[f"LKF{c % 2}"], lkf_after[cc])
                if cc >= 1:
                    t.wait_ge(S["KVE"], kve_after[cc - 1])
                kfc = kf[c % 2]
                for h in list(range(14)) + [14, 15]:
                    ee = h // 2
                    odd = h % 2
                    if h == 8:
                        t.wait_ge(S["MEV"], mev_after[cc])
                    if h >= 14:
                        nb, off = spair[odd], 447
                        if h == 14:
                            for bb in spair:
                                t.wait_ge(S[f"BF{bb}"], wbk[bb])
                    else:
                        nb, off = mpair[odd], (h // 2) * 65
                    pr = h // 2
                    kvb = 0 if pr < 4 else 1
                    kvcol = (pr % 4) * 65
                    kvs = kvcol + 260 * kvb
                    kvpart = slice(0, 64) if odd == 0 else slice(64, 128)
                    tile_col = (0, 0) if odd == 0 else (0, 64)
                    tile_row = (0, 0) if odd == 0 else (64, 0)
                    qsl = slice(ee * ST + c * 128, ee * ST + (c + 1) * 128)
                    hpart = slice(0, 64) if odd == 0 else slice(64, 128)
                    ams = slice(am_off(h), am_off(h) + 128)
                    vsl = slice(c * 1040 + h * 65, c * 1040 + (h + 1) * 65)
                    opener = h in (0, 1, 14, 15)
                    closer = h in (12, 13, 14, 15)
                    if not first:
                        t.matmul(B[nb][:, off:off + 65], qfT[hpart, qsl],
                                 kv_sb[kvpart, kvs:kvs + 65],
                                 start=opener, stop=False, tile_position=tile_row)
                    t.matmul(B[nb][:, off:off + 65], amb[:, ams], vbuf[:, vsl],
                             start=(first and opener), stop=closer)
                    kv_open = first and h in (0, 1, 8, 9)
                    kv_close = (cb == CCH - 1) and h in (6, 7, 14, 15)
                    t.matmul(B[kvb][kvpart, kvcol:kvcol + 65], kfc[:, h * 64:(h + 1) * 64],
                             vbuf[:, vsl],
                             start=kv_open, stop=kv_close, tile_position=tile_col,
                             skip_group_check=True)
                inst = t.matmul(B[4][:, 300:428], amb[:, 0:128], kf[c % 2][:, 0:128],
                                start=True, stop=True)
                inst.then_inc(S["NSC"], 1)
            p.emit("tensor", t_ns)
            nsc_after[cc] = my_nsc

            my_rcp = p.add("RCP", 1)

            def v_rc(v, cc=cc, mpair=mpair, spair=spair):
                v.wait_ge(S["NSC"], nsc_after[cc])
                v.tensor_scalar_add(
                    dtmp[:, 0:7],
                    B[mpair[0]][:, 0:455].rearrange("p (n v) -> p n v", v=65)[:, :, 64:65],
                    EPS_ATTN)
                v.tensor_scalar_add(
                    dtmp[:, 7:14],
                    B[mpair[1]][:, 0:455].rearrange("p (n v) -> p n v", v=65)[:, :, 64:65],
                    EPS_ATTN)
                v.tensor_scalar_add(dtmp[:, 14:15], B[spair[0]][:, 511:512], EPS_ATTN)
                v.tensor_scalar_add(dtmp[:, 15:16], B[spair[1]][:, 511:512], EPS_ATTN)
                v.reciprocal(recip[:, 0:8], dtmp[:, 0:8])
                v.reciprocal(recip[:, 8:16], dtmp[:, 8:16]).then_inc(S["RCP"], 1)
            p.emit("vector", v_rc)
            rcp_after[cc] = my_rcp

            if debug and cc < 4:
                def g_dbg(g, cc=cc):
                    g.wait_ge(S["RCP"], rcp_after[cc])
                    g.dma_start(out=dbg_h[:, cc * 16:(cc + 1) * 16], in_=recip[:, :]).then_inc(S["GDB"], 16)
                p.emit("gpsimd", g_dbg)
                p.add("GDB", 16)
                gdb_now = p.cnt("GDB")

                def v_rc_gate(v, gdb_now=gdb_now):
                    v.wait_ge(S["GDB"], gdb_now)
                p.emit("vector", v_rc_gate)

            my_div = p.add("DIV", 1)
            prev_gao = gao_after.get(i - 1, 0)

            def a_dv(s, cc=cc, c=c, prev_gao=prev_gao, mpair=mpair, spair=spair):
                s.wait_ge(S["NSC"], nsc_after[cc])
                s.wait_ge(S["RCP"], rcp_after[cc])
                if c == 0 and prev_gao > 0:
                    s.wait_ge(S["GAO"], prev_gao)
                for h in (14, 15, 12, 13, 0, 1, 2, 3, 4, 5, 6, 7, 8, 9, 10, 11):
                    if h >= 14:
                        nb, off, rcol = spair[h % 2], 447, h
                    else:
                        nb = mpair[h % 2]
                        off = (h // 2) * 65
                        rcol = (h // 2) + 7 * (h % 2)
                    inst = s.activation(out=aob[:, c * D + h * 64: c * D + (h + 1) * 64],
                                        in_=B[nb][:, off:off + 64], func=AF.Copy,
                                        scale=recip[:, rcol:rcol + 1])
                    if h >= 14:
                        inst.then_inc(S[f"BF{spair[h % 2]}"], 1)
                inst.then_inc(S["DIV"], 1)
            p.emit("scalar", a_dv)
            div_after[cc] = my_div
            for bb in (2, 3, 6, 7):
                bank[bb]["frees"] += 1

            def a_dvf(s, mpair=mpair):
                for j, bb in enumerate(mpair):
                    s.activation(out=jk2[:, j:j + 1], in_=B[bb][:, 447:448],
                                 func=AF.Copy).then_inc(S[f"BF{bb}"], 1)
            p.emit("scalar", a_dvf)

            my_kve = p.add("KVE", 1)

            def v_kv(v, cc=cc):
                v.wait_ge(S["NSC"], nsc_after[cc])
                v.tensor_copy(out=kv_sb[:, 0:260], in_=B[0][:, 0:260])
                v.tensor_copy(out=kv_sb[:, 260:520], in_=B[1][:, 0:260]).then_inc(S["KVE"], 1)
            p.emit("vector", v_kv)
            kve_after[cc] = my_kve

            if debug and cc == 0:
                def g_dbg2(g):
                    g.wait_ge(S["KVE"], 1)
                    g.dma_start(out=dbg_h[:, 64:584], in_=kv_sb[:, :]).then_inc(S["GDB"], 16)
                    g.wait_ge(S["MEV"], 4)
                    g.dma_start(out=dbg_h[:, 584:584 + 2048], in_=amT[0][:, :]).then_inc(S["GDB"], 16)
                p.emit("gpsimd", g_dbg2)
                p.add("GDB", 32)
                gdb_now2 = p.cnt("GDB")

                def v_kv_gate(v, gdb_now2=gdb_now2):
                    v.wait_ge(S["GDB"], gdb_now2)
                p.emit("vector", v_kv_gate)

        for c in range(4):
            core_at(c)
            core_ns(c)

        def g_ao(g, i=i, t0=t0):
            g.wait_ge(S["DIV"], div_after[i * 4 + 3])
            for tt in range(4):
                g.dma_start(out=ao_h[t0 + tt * 128: t0 + (tt + 1) * 128, :],
                            in_=aob[:, tt * D:(tt + 1) * D]).then_inc(S["GAO"], 16)
        p.emit("gpsimd", g_ao)
        gao_after[i] = p.add("GAO", 16 * 4)

        if 2 <= i + 1 < NST:
            stage_sXT(i + 1)

    def stage_bW(i):
        t0 = i * ST

        for tt in range(4):
            tile = i * 4 + tt
            tb = tt % 2

            def s_xr(sy, i=i, t0=t0, tt=tt, tb=tb, tile=tile):
                if tile >= 2:
                    sy.wait_ge(S["VLN"], xres_free[tile - 2])
                sy.dma_start(out=xres[tb][:, :],
                             in_=x_in[t0 + tt * 128: t0 + (tt + 1) * 128, :]).then_inc(S[f"LXR{tb}"], 16)
            p.emit("sync", s_xr)
            lxr_after[tile] = p.add(f"LXR{tb}", 16)

            fills = []
            for half in range(2):
                bid = 2 + half
                w_bf = bank[bid]["frees"]
                my_fill = p.add("MMF", 1)
                fills.append(my_fill)

                def t_wo(t, tt=tt, half=half, bid=bid, w_bf=w_bf, i=i):
                    t.wait_ge(S[f"BF{bid}"], w_bf)
                    t.wait_ge(S["LAO"], lao_after[i])
                    for kk in range(8):
                        inst = t.matmul(B[bid][:, :],
                                        aoT[:, kk * ST + tt * 128: kk * ST + (tt + 1) * 128],
                                        wT["o"][:, kk * D + half * 512: kk * D + (half + 1) * 512],
                                        start=(kk == 0), stop=(kk == 7))
                    inst.then_inc(S["MMF"], 1)
                p.emit("tensor", t_wo)

            my_vln = p.add("VLN", 1)
            prev_aln = aln_lnexp.get(tile - 2, 0)

            def v_u(v, fills=tuple(fills), tb=tb, tile=tile, prev_aln=prev_aln):
                v.wait_ge(S["MMF"], fills[1])
                v.wait_ge(S[f"LXR{tb}"], lxr_after[tile])
                if prev_aln > 0:
                    v.wait_ge(S["ALN"], prev_aln)
                v.tensor_add(ubuf[tb][:, 0:512], xres[tb][:, 0:512], B[2][:, :]).then_inc(S["BF2"], 1)
                v.tensor_add(ubuf[tb][:, 512:1024], xres[tb][:, 512:1024], B[3][:, :]).then_inc(S["BF3"], 1)
                v.bn_stats(stat[tb][:, 0:6], ubuf[tb][:, 0:512])
                v.bn_stats(stat[tb][:, 6:12], ubuf[tb][:, 512:1024])
                v.tensor_copy(out=vjk[:, :], in_=iotap[:, :])
                v.bn_aggr(mv[tb][:, :], stat[tb][:, :]).then_inc(S["VLN"], 1)
            p.emit("vector", v_u)
            bank[2]["frees"] += 1
            bank[3]["frees"] += 1
            xres_free[tile] = my_vln

            my_aln = p.add("ALN", 1)
            aln_lnexp[tile] = my_aln

            def a_ln(s, my_vln=my_vln, tb=tb):
                s.wait_ge(S["VLN"], my_vln)
                s.activation(out=lnt[tb][:, 0:1], in_=mv[tb][:, 1:2], func=AF.Ln, bias=cst_eps[:, :])
                s.activation(out=lnt[tb][:, 1:2], in_=lnt[tb][:, 0:1], func=AF.Exp, scale=cst_mh[:, :])\
                 .then_inc(S["ALN"], 1)
            p.emit("scalar", a_ln)

            my_vln2 = p.add("VLN", 1)
            u_free[tile] = my_vln2

            def v_x1(v, my_aln=my_aln, tb=tb, tile=tile):
                v.wait_ge(S["ALN"], my_aln)
                if tile >= 2:
                    v.wait_ge(S["GX1"], x1_free.get(tile - 2, 0))
                v.tensor_scalar(x1b[tb][:, :], ubuf[tb][:, :], mv[tb][:, 0:1], lnt[tb][:, 1:2],
                                ALU.subtract, ALU.mult).then_inc(S["VLN"], 1)
            p.emit("vector", v_x1)

            my_aln2 = p.add("ALN", 1)

            def a_hi(s, my_vln2=my_vln2, tb=tb):
                s.wait_ge(S["VLN"], my_vln2)
                s.activation(out=x1hi[tb][:, :], in_=x1b[tb][:, :], func=AF.Copy)\
                 .then_inc(S["ALN"], 1)
            p.emit("scalar", a_hi)

            my_vln3 = p.add("VLN", 1)
            vlo_after[tile] = my_vln3

            def v_lo(v, my_aln2=my_aln2, tb=tb):
                v.wait_ge(S["ALN"], my_aln2)
                v.tensor_sub(x1lo[tb][:, :], x1b[tb][:, :], x1hi[tb][:, :]).then_inc(S["VLN"], 1)
            p.emit("vector", v_lo)

            gx1_pre = p.cnt("GX1")

            def g_x1(g, my_vln3=my_vln3, tb=tb, t0=t0, tt=tt, gx1_pre=gx1_pre):
                g.wait_ge(S["VLN"], my_vln3)
                g.wait_ge(S["GX1"], gx1_pre)
                r0 = t0 + tt * 128
                g.dma_start(out=x1_h[r0:r0 + 128, :], in_=x1b[tb][:, :]).then_inc(S["GX1"], 16)
                g.dma_start(out=x1hi_h[r0:r0 + 128, :], in_=x1hi[tb][:, :]).then_inc(S["GX1"], 16)
                g.dma_start(out=x1lo_h[r0:r0 + 128, :], in_=x1lo[tb][:, :]).then_inc(S["GX1"], 16)
            p.emit("gpsimd", g_x1)
            x1_free[tile] = p.add("GX1", 48)
        wo_cons[i] = p.cnt("MMF")
        gx1_after[i] = p.cnt("GX1")

    def stage_bL(i):
        for tt in range(4):
            col = i * 4 + tt
            bid = 2 + (tt % 2)
            w_bf = bank[bid]["frees"]
            my_fill = p.add("MMF", 1)

            def t_lg(t, tt=tt, i=i, w_bf=w_bf, bid=bid):
                t.wait_ge(S[f"BF{bid}"], w_bf)
                t.wait_ge(S["LHT"], lht_after[i])
                for kk in range(8):
                    tsl = slice(kk * ST + tt * 128, kk * ST + (tt + 1) * 128)
                    t.matmul(B[bid][:, 0:8], hiT[:, tsl], wg_hl[:, kk * 16:kk * 16 + 8],
                             start=(kk == 0), stop=False)
                    t.matmul(B[bid][:, 0:8], hiT[:, tsl], wg_hl[:, kk * 16 + 8:kk * 16 + 16],
                             start=False, stop=False)
                    inst = t.matmul(B[bid][:, 0:8], loT[:, tsl], wg_hl[:, kk * 16:kk * 16 + 8],
                                    start=False, stop=(kk == 7))
                inst = t.matmul(B[bid][:, 128:256], hiT[:, tsl], hiT[:, 0:128],
                                start=True, stop=True)
                inst.then_inc(S["MMF"], 1)
            p.emit("tensor", t_lg)

            my_vlg = p.add("VLG", 1)

            def v_lg(v, my_fill=my_fill, bid=bid):
                v.wait_ge(S["MMF"], my_fill)
                v.reduce_max(smt[:, 0:1], B[bid][:, 0:8], axis=AX.X, negate=True)
                v.reduce_max(smt[:, 2:3], B[bid][:, 0:8], axis=AX.X).then_inc(S["VLG"], 1)
            p.emit("vector", v_lg)

            my_aln3 = p.add("ALN", 1)

            def a_sm(s, my_vlg=my_vlg, bid=bid):
                s.wait_ge(S["VLG"], my_vlg)
                s.activation(out=lge[:, :], in_=B[bid][:, 0:8], func=AF.Exp,
                             bias=smt[:, 0:1], accum_out=smt[:, 1:2]).then_inc(S["ALN"], 1)
            p.emit("scalar", a_sm)

            def v_gb(v, my_aln3=my_aln3, col=col, bid=bid):
                v.wait_ge(S["ALN"], my_aln3)
                v.reciprocal(gate_c[:, col:col + 1], smt[:, 1:2])
                v.tensor_single_scalar(ohb[:, :], B[bid][:, 0:8], smt[:, 2:3],
                                       ALU.is_equal).then_inc(S[f"BF{bid}"], 1)
                v.tensor_copy(out=vjk[:, :], in_=iotap[:, :])
                v.scalar_tensor_tensor(lge[:, :], ohb[:, :], 1.0, iota8r[:, :],
                                       ALU.mult, ALU.mult,
                                       accum_out=best_c[:, col:col + 1]).then_inc(S["VLG"], 1)
            p.emit("vector", v_gb)
            bank[bid]["frees"] += 1
            p.add("VLG", 1)
        lg_cons[i] = p.cnt("MMF")

    for i in range(NST):
        stage_fQ(i)
        if i >= 2:
            stage_bL(i - 2)
        stage_fC(i)
        if i >= 1:
            stage_bW(i - 1)
    stage_sAO(NST - 1)
    stage_sHT(NST - 2)
    stage_bW(NST - 1)
    stage_bL(NST - 2)
    stage_sHT(NST - 1)
    stage_bL(NST - 1)

    es1.close()
    trb = E(nc.sbuf_tensor("trb", [128, 128], BF16))
    brep = [E(nc.sbuf_tensor(f"brep{j}", [8, FCH], BF16)) for j in range(2)]
    bufM = E(nc.sbuf_tensor("bufM", [8, FCH], F32))
    bufS = E(nc.sbuf_tensor("bufS", [8, FCH], F32))
    bufC = [E(nc.sbuf_tensor(f"bufC{j}", [8, FCH], F32)) for j in range(2)]
    run8 = E(nc.sbuf_tensor("run8", [8, 1], F32))
    carry8 = E(nc.sbuf_tensor("carry8", [8, 1], F32))
    ag_sb = E(nc.sbuf_tensor("ag_sb", [8 * n_cores, 1], F32))
    cntf = [E(nc.sbuf_tensor(f"cntf{j}", [1, FCH], F32)) for j in range(2)]
    gatef = [E(nc.sbuf_tensor(f"gatef{j}", [1, FCH], BF16)) for j in range(2)]
    gf = [E(nc.sbuf_tensor(f"gf{j}", [1, FCH], F32)) for j in range(2)]
    g4 = [E(nc.sbuf_tensor(f"g4_{j}", [128, 4], F32)) for j in range(2)]
    we_sb = E(nc.sbuf_tensor("we_sb", [128, 8 * D], BF16))
    hiT2 = E(nc.sbuf_tensor("hiT2", [128, 8 * ST], BF16))
    hiP = [hiT, hiT2, loT]
    lg4_st = {}

    def stage_p2load(i):
        t0 = i * ST
        hib = hiP[i % 3]

        def s_p2(sy, i=i, t0=t0, hib=hib):
            if i < 3:
                sy.wait_ge(S["MMF"], lg_cons[NST - 1])
            else:
                sy.wait_ge(S["MMF"], eo_cons[i - 3])
            for kk in range(8):
                sy.dma_start(out=hib[:, kk * ST:(kk + 1) * ST],
                             in_=x1hi_h[t0:t0 + ST, kk * 128:(kk + 1) * 128],
                             transpose=True).then_inc(S["LP2"], 16)
        p.emit("sync", s_p2)
        lp2_st[i] = p.add("LP2", 16 * 8)

    def emit_s_we():
        def s_we(sy):
            sy.wait_ge(S["MMF"], lg_cons[NST - 1])
            sy.dma_start(out=we_sb.ap().rearrange("p (kk e) -> p kk e", kk=8),
                         in_=wbf_h["e"].rearrange("(kk p) e -> p kk e", p=128)).then_inc(S["LP2"], 16)
        p.emit("sync", s_we)
        return p.add("LP2", 16)

    vlg_all = p.cnt("VLG")

    def v_cols(v):
        v.wait_ge(S["VLG"], vlg_all)
        v.tensor_copy(out=vjk[:, :], in_=iotap[:, :])
        v.tensor_copy(out=gate_cb[:, 0:NTT], in_=gate_c[:, :])
        v.tensor_copy(out=best_cb[:, 0:NTT], in_=best_c[:, :]).then_inc(S["VTL"], 1)
    p.emit("vector", v_cols)
    p.add("VTL", 1)

    def g_cols(g):
        g.wait_ge(S["VTL"], 1)
        g.memset(run8[:, :], 0.0)
        g.dma_start(out=bcol_h[:, :], in_=best_cb[:, :]).then_inc(S["GTL"], 16)
        g.dma_start(out=gcol_h[:, :], in_=gate_cb[:, :]).then_inc(S["GTL"], 16)
    p.emit("gpsimd", g_cols)
    p.add("GTL", 32)

    def s_tr(sy):
        sy.wait_ge(S["GTL"], 32)
        sy.dma_start(out=trb[:, :], in_=bcol_h[:, :], transpose=True).then_inc(S["LTL"], 16)
    p.emit("sync", s_tr)
    p.add("LTL", 16)

    def g_tok(g):
        g.wait_ge(S["LTL"], 16)
        g.dma_start(out=btok_h[:, :], in_=trb[:, :]).then_inc(S["GTL"], 16)
    p.emit("gpsimd", g_tok)
    p.add("GTL", 16)

    def s_tr2(sy):
        sy.wait_ge(S["GTL"], 48)
        sy.dma_start(out=trb[:, :], in_=gcol_h[:, :], transpose=True).then_inc(S["LTL"], 16)
    p.emit("sync", s_tr2)
    p.add("LTL", 16)

    def g_tok2(g):
        g.wait_ge(S["LTL"], 32)
        g.dma_start(out=gtok_h[:, :], in_=trb[:, :]).then_inc(S["GTL"], 16)
    p.emit("gpsimd", g_tok2)
    p.add("GTL", 16)
    gtl_tok = p.cnt("GTL")

    btok_flat = btok_h.rearrange("a b -> (a b)")
    gtok_flat = gtok_h.rearrange("a b -> (a b)")

    stage_p2load(0)

    vtl_eq1 = {}
    for k in range(NFC):
        w_vtl = vtl_eq1.get(k - 2, 0)

        def s_br(sy, k=k, w_vtl=w_vtl):
            sy.wait_ge(S["GTL"], gtl_tok)
            if w_vtl:
                sy.wait_ge(S["VTL"], w_vtl)
            sy.dma_start(out=brep[k % 2][:, :],
                         in_=btok_flat[k * FCH:(k + 1) * FCH]
                         .rearrange("(a t) -> a t", a=1).partition_broadcast(8)).then_inc(S["LTL"], 16)
        p.emit("sync", s_br)
        my_ltl = p.add("LTL", 16)

        def v_scan(v, my_ltl=my_ltl, k=k):
            v.wait_ge(S["LTL"], my_ltl)
            v.wait_ge(S["GPRE"], 1)
            v.tensor_single_scalar(bufM[:, :], brep[k % 2][:, :], iota8[:, :],
                                   ALU.is_equal).then_inc(S["VTL"], 1)
            v.tensor_copy(out=vjk[:, :], in_=iotap[:, :])
            v.tensor_tensor_scan(out=bufS[:, :], data0=bufM[:, :], data1=bufM[:, :],
                                 initial=run8[:, :], op0=ALU.add, op1=ALU.bypass)
            v.tensor_copy(out=vjk[:, :], in_=iotap[:, :])
            v.tensor_copy(out=run8[:, :], in_=bufS[:, FCH - 1:FCH]).then_inc(S["VTL"], 1)
        p.emit("vector", v_scan)
        vtl_eq1[k] = p.add("VTL", 1)
        p.add("VTL", 1)
    vtl_sweep1 = p.cnt("VTL")

    gtl_ccin = p.cnt("GTL") + 16

    def g_cc(g):
        g.wait_ge(S["VTL"], vtl_sweep1)
        g.dma_start(out=cc_in[:, :], in_=run8[:, :]).then_inc(S["GTL"], 16)
        g.wait_ge(S["GTL"], gtl_ccin)
        g.collective_compute("AllGather", ALU.bypass, ins=[cc_in[:, :]], outs=[cc_out[:, :]],
                             replica_groups=[list(range(n_cores))]).then_inc(S["CCD"], 1)
    p.emit("gpsimd", g_cc)
    p.add("GTL", 16)
    p.add("CCD", 1)

    def s_ag(sy):
        sy.wait_ge(S["CCD"], 1)
        sy.dma_start(out=ag_sb[:, :], in_=cc_out[:, :]).then_inc(S["LTL"], 16)
    p.emit("sync", s_ag)
    ltl_ag = p.add("LTL", 16)

    lp2_we = emit_s_we()
    if NST > 1:
        stage_p2load(1)
    if NST > 2:
        stage_p2load(2)

    carry_fill = p.add("MMF", 1)

    def t_carry(t):
        t.wait_ge(S["LTL"], ltl_ag)
        t.matmul(B[4][0:8, 0:1], sel_sb[:, :], ag_sb[:, :], start=True, stop=True)
        t.matmul(B[4][0:1, 64:192], ones8[:, :], bufM[:, 0:128],
                 start=True, stop=True).then_inc(S["MMF"], 1)
    p.emit("tensor", t_carry)

    my_vtl = p.add("VTL", 1)

    def v_carry(v):
        v.wait_ge(S["MMF"], carry_fill)
        v.tensor_copy(out=carry8[:, :], in_=B[4][0:8, 0:1]).then_inc(S["BF4"], 1)
        v.tensor_copy(out=vjk[:, :], in_=iotap[:, :])
        v.tensor_copy(out=run8[:, :], in_=carry8[:, :]).then_inc(S["VTL"], 1)
    p.emit("vector", v_carry)

    vtl_eq2 = {}
    vg_vtl = {}
    sumfill = {}
    gg_gp2 = {}
    for k in range(NFC):
        w_vtl = vtl_eq2.get(k - 2, 0)
        w_vg = vg_vtl.get(k - 2, 0)

        def s_br2(sy, k=k, w_vtl=w_vtl, w_vg=w_vg):
            sy.wait_ge(S["VTL"], max(w_vtl, w_vg, vtl_eq1[NFC - 1] if k < 2 else 0))
            sy.dma_start(out=brep[k % 2][:, :],
                         in_=btok_flat[k * FCH:(k + 1) * FCH]
                         .rearrange("(a t) -> a t", a=1).partition_broadcast(8)).then_inc(S["LTL"], 16)
            sy.dma_start(out=gatef[k % 2][:, :],
                         in_=gtok_flat[k * FCH:(k + 1) * FCH]
                         .rearrange("(a t) -> a t", a=1)).then_inc(S["LTL"], 16)
        p.emit("sync", s_br2)
        my_ltl = p.add("LTL", 32)

        def v_scan2(v, my_ltl=my_ltl, k=k, w_sum=sumfill.get(k - 2, 0)):
            v.wait_ge(S["LTL"], my_ltl)
            v.tensor_single_scalar(bufM[:, :], brep[k % 2][:, :], iota8[:, :],
                                   ALU.is_equal).then_inc(S["VTL"], 1)
            v.tensor_copy(out=vjk[:, :], in_=iotap[:, :])
            v.tensor_tensor_scan(out=bufS[:, :], data0=bufM[:, :], data1=bufM[:, :],
                                 initial=run8[:, :], op0=ALU.add, op1=ALU.bypass)
            v.tensor_copy(out=vjk[:, :], in_=iotap[:, :])
            v.tensor_copy(out=run8[:, :], in_=bufS[:, FCH - 1:FCH]).then_inc(S["VTL"], 1)
            if w_sum:
                v.wait_ge(S["MMF"], w_sum)
            v.tensor_mul(bufC[k % 2][:, :], bufM[:, :], bufS[:, :]).then_inc(S["VTL"], 1)
        p.emit("vector", v_scan2)
        p.add("VTL", 3)
        vtl_eq2[k] = p.cnt("VTL") - 2
        my_vtl2 = p.cnt("VTL")

        for j in range(FCH // 512):
            bk = 4 + j
            my_fill = p.add("MMF", 1)

            def t_sum(t, j=j, k=k, bk=bk, my_vtl2=my_vtl2):
                t.wait_ge(S[f"BF{bk}"], (1 if bk == 4 else 0) + max(0, k))
                t.wait_ge(S["VTL"], my_vtl2)
                t.matmul(B[bk][0:1, :], ones8[:, :], bufC[k % 2][:, j * 512:(j + 1) * 512],
                         start=True, stop=True).then_inc(S["MMF"], 1)
            p.emit("tensor", t_sum)

            def v_cf(v, my_fill=my_fill, j=j, k=k, bk=bk):
                v.wait_ge(S["MMF"], my_fill)
                v.tensor_copy(out=cntf[k % 2][:, j * 512:(j + 1) * 512],
                              in_=B[bk][0:1, :]).then_inc(S[f"BF{bk}"], 1)
            p.emit("vector", v_cf)
        sumfill[k] = p.cnt("MMF")

        my_vtl3 = p.add("VTL", 1)

        def v_g(v, k=k, w_gp2=gg_gp2.get(k - 2, 0)):
            if w_gp2:
                v.wait_ge(S["GP2"], w_gp2)
            v.tensor_copy(out=vjk[:, :], in_=iotap[:, :])
            v.tensor_single_scalar(bufM[0:1, :], cntf[k % 2][:, :], float(CAP) + 0.49, ALU.is_le)
            v.tensor_copy(out=vjk[:, :], in_=iotap[:, :])
            v.tensor_mul(gf[k % 2][:, :], bufM[0:1, :], gatef[k % 2][:, :]).then_inc(S["VTL"], 1)
        p.emit("vector", v_g)
        vg_vtl[k] = my_vtl3

        def g_g(g, k=k, my_vtl3=my_vtl3):
            g.wait_ge(S["VTL"], my_vtl3)
            g.dma_start(out=g_h[0:1, k * FCH:(k + 1) * FCH], in_=gf[k % 2][:, :]).then_inc(S["GP2"], 16)
        p.emit("gpsimd", g_g)
        gg_gp2[k] = p.add("GP2", 16)
    gp2_all = p.cnt("GP2")

    for i in range(NST):
        t0 = i * ST
        hib = hiP[i % 3]

        def s_g4(sy, i=i, t0=t0):
            sy.wait_ge(S["GP2"], 16 * (i // 2 + 1))
            if i >= 2:
                sy.wait_ge(S["VP2"], vp2_st[i - 2])
            sy.dma_start(out=g4[i % 2][:, :],
                         in_=g_h[0, t0:t0 + ST].rearrange("(tt p) -> p tt", p=128),
                         allow_slow_non_contiguous=True).then_inc(S["LG4"], 16)
        p.emit("sync", s_g4)
        lg4_st[i] = p.add("LG4", 16)

        for tt in range(4):
            tile = i * 4 + tt
            tb = tt % 2
            pr = (2, 3) if tile % 2 == 0 else (6, 7)

            def s_x1r(sy, i=i, t0=t0, tt=tt, tb=tb, tile=tile):
                if tile >= 2:
                    sy.wait_ge(S["VP2"], p2_tile_free[tile - 2])
                sy.dma_start(out=xres[tb][:, :],
                             in_=x1_h[t0 + tt * 128: t0 + (tt + 1) * 128, :]).then_inc(S[f"LXR{tb}"], 16)
            p.emit("sync", s_x1r)
            lp2_x1[tile] = p.add(f"LXR{tb}", 16)

            fills = []
            for half in range(2):
                bid = pr[half]
                w_bf = bank[bid]["frees"]
                my_fill = p.add("MMF", 1)
                fills.append(my_fill)

                def t_eo(t, tt=tt, half=half, bid=bid, w_bf=w_bf, i=i, hib=hib):
                    t.wait_ge(S[f"BF{bid}"], w_bf)
                    t.wait_ge(S["LP2"], max(lp2_st[i], lp2_we))
                    for kk in range(8):
                        t.matmul(B[bid][:, :],
                                 hib[:, kk * ST + tt * 128: kk * ST + (tt + 1) * 128],
                                 we_sb[:, kk * D + half * 512: kk * D + (half + 1) * 512],
                                 start=(kk == 0), stop=False)
                    inst = t.matmul(B[bid][:, :], ones1b[:, :],
                                    be_b[0:1, half * 512:(half + 1) * 512],
                                    start=False, stop=True)
                    inst.then_inc(S["MMF"], 1)
                p.emit("tensor", t_eo)

            my_vp2 = p.add("VP2", 1)
            prev_ap2 = ap2_lnexp.get(tile - 2, 0)

            def v_u2(v, fills=tuple(fills), tb=tb, tile=tile, tt=tt, i=i,
                     prev_ap2=prev_ap2, pr=pr):
                v.wait_ge(S["MMF"], fills[1])
                v.wait_ge(S["LG4"], lg4_st[i])
                if prev_ap2 > 0:
                    v.wait_ge(S["AP2"], prev_ap2)
                v.wait_ge(S[f"LXR{tb}"], lp2_x1[tile])
                v.scalar_tensor_tensor(ubuf[tb][:, 0:512], B[pr[0]][:, :], g4[i % 2][:, tt:tt + 1],
                                       xres[tb][:, 0:512], ALU.mult, ALU.add).then_inc(S[f"BF{pr[0]}"], 1)
                v.scalar_tensor_tensor(ubuf[tb][:, 512:1024], B[pr[1]][:, :], g4[i % 2][:, tt:tt + 1],
                                       xres[tb][:, 512:1024], ALU.mult, ALU.add).then_inc(S[f"BF{pr[1]}"], 1)
                v.bn_stats(stat[tb][:, 0:6], ubuf[tb][:, 0:512])
                v.bn_stats(stat[tb][:, 6:12], ubuf[tb][:, 512:1024])
                v.tensor_copy(out=vjk[:, :], in_=iotap[:, :])
                v.bn_aggr(mv[tb][:, :], stat[tb][:, :]).then_inc(S["VP2"], 1)
            p.emit("vector", v_u2)
            bank[pr[0]]["frees"] += 1
            bank[pr[1]]["frees"] += 1

            my_ap2 = p.add("AP2", 1)
            ap2_lnexp[tile] = my_ap2

            def a_ln2(s, my_vp2=my_vp2, tb=tb):
                s.wait_ge(S["VP2"], my_vp2)
                s.activation(out=lnt[tb][:, 0:1], in_=mv[tb][:, 1:2], func=AF.Ln, bias=cst_eps[:, :])
                s.activation(out=lnt[tb][:, 1:2], in_=lnt[tb][:, 0:1], func=AF.Exp, scale=cst_mh[:, :])\
                 .then_inc(S["AP2"], 1)
            p.emit("scalar", a_ln2)

            my_vp22 = p.add("VP2", 1)
            p2_tile_free[tile] = my_vp22

            def v_y(v, my_ap2=my_ap2, tb=tb, tile=tile):
                v.wait_ge(S["AP2"], my_ap2)
                if tile >= 2:
                    v.wait_ge(S["GY"], gy_after.get(tile - 2, 0))
                v.tensor_scalar(x1b[tb][:, :], ubuf[tb][:, :], mv[tb][:, 0:1], lnt[tb][:, 1:2],
                                ALU.subtract, ALU.mult).then_inc(S["VP2"], 1)
            p.emit("vector", v_y)

            gy_pre = p.cnt("GY")

            def g_y(g, my_vp22=my_vp22, tb=tb, t0=t0, tt=tt, gy_pre=gy_pre):
                g.wait_ge(S["VP2"], my_vp22)
                g.wait_ge(S["GY"], gy_pre)
                g.dma_start(out=y_out[t0 + tt * 128: t0 + (tt + 1) * 128, :],
                            in_=x1b[tb][:, :]).then_inc(S["GY"], 16)
            p.emit("gpsimd", g_y)
            gy_after[tile] = p.add("GY", 16)
        vp2_st[i] = p.cnt("VP2")
        eo_cons[i] = p.cnt("MMF")
        if i + 3 < NST:
            stage_p2load(i + 3)

    gy_all = p.cnt("GY")

    def s_fin(sy):
        sy.wait_ge(S["GY"], gy_all)
    p.emit("sync", s_fin)

    for name in p.counts:
        S[name] = es.enter_context(nc.semaphore(name))
    with nc.Block() as block:
        for eng_name in Prog.ENGS:
            def mk(eng_name):
                def body(eng):
                    for fn in p.items[eng_name]:
                        fn(eng)
                return body
            getattr(block, eng_name)(mk(eng_name))
    es.close()
    return nc


def shard_inputs(inputs, n_cores, bpc):
    x = np.ascontiguousarray(np.asarray(inputs["L0_attn_x"], dtype=np.float32))
    Bfull = x.shape[0]
    x2 = x.reshape(Bfull * T, D)
    maps = []
    for c in range(n_cores):
        sel = np.zeros((8 * n_cores, 8), np.float32)
        for r in range(c):
            sel[r * 8:(r + 1) * 8] = np.eye(8, dtype=np.float32)
        maps.append({
            "x": np.ascontiguousarray(x2[c * bpc * T:(c + 1) * bpc * T]),
            "wq": np.ascontiguousarray(np.asarray(inputs["L0_attn_Wq"], np.float32)),
            "wk": np.ascontiguousarray(np.asarray(inputs["L0_attn_Wk"], np.float32)),
            "wv": np.ascontiguousarray(np.asarray(inputs["L0_attn_Wv"], np.float32)),
            "wo": np.ascontiguousarray(np.asarray(inputs["L0_attn_Wo"], np.float32)),
            "we": np.ascontiguousarray(np.asarray(inputs["L0_ffn_w_expert"], np.float32)),
            "wg": np.ascontiguousarray(np.asarray(inputs["L0_ffn_w_gate"], np.float32)),
            "be": np.ascontiguousarray(np.asarray(inputs["L0_ffn_b_expert"], np.float32).reshape(1, D)),
            "sel": sel,
        })
    return maps


N_CORES = 8
BPC = 4
_CACHE = {}


def kernel(**inputs):
    if "nc" not in _CACHE:
        nc = bass.Bass("TRN2", target_bir_lowering=False, debug=False,
                       detect_race_conditions=False)
        build_program(nc, BPC, N_CORES)
        _CACHE["nc"] = nc
    nc = _CACHE["nc"]
    from concourse.bass_utils import run_bass_kernel_spmd
    maps = shard_inputs(inputs, N_CORES, BPC)
    res = run_bass_kernel_spmd(nc, maps, core_ids=list(range(N_CORES)))
    y = np.concatenate([np.asarray(res.results[c]["y"]) for c in range(N_CORES)], axis=0)
    return y.reshape(32, T, D)


# version 16 (final)
# speedup vs baseline: 1.2517x; 1.0002x over previous
import contextlib
import numpy as np
import concourse.bass as bass
import concourse.mybir as mybir

F32 = mybir.dt.float32
BF16 = mybir.dt.bfloat16
AF = mybir.ActivationFunctionType
ALU = mybir.AluOpType
AX = mybir.AxisListType

D = 1024
T = 2048
ST = 512
EPS_ATTN = 1e-6
EPS_LN = 1e-5
CAP = 100
FCH = 1024
RELAX_NS = 1


def am_off(h):
    hc = h // 8
    slot = (h % 8) // 2
    return hc * 1024 + (h % 2) * 512 + slot * 128


class Prog:
    ENGS = ("sync", "gpsimd", "tensor", "vector", "scalar")

    def __init__(self):
        self.items = {e: [] for e in self.ENGS}
        self.counts = {}

    def sem(self, name):
        self.counts.setdefault(name, 0)
        return name

    def add(self, name, n=1):
        self.counts[name] = self.counts.get(name, 0) + n
        return self.counts[name]

    def cnt(self, name):
        return self.counts.get(name, 0)

    def emit(self, eng, fn):
        self.items[eng].append(fn)


def build_program(nc, BPC, n_cores, debug=False):
    NT = BPC * T
    NST = NT // ST
    NTT = NT // 128
    STB = T // ST
    CCH = T // 128
    NFC = max(1, NT // FCH)

    x_in = nc.dram_tensor("x", [NT, D], F32, kind="ExternalInput").ap()
    w_in = {k: nc.dram_tensor(f"w{k}", [D, D], F32, kind="ExternalInput").ap()
            for k in ("q", "k", "v", "o", "e")}
    wg_in = nc.dram_tensor("wg", [D, 8], F32, kind="ExternalInput").ap()
    be_in = nc.dram_tensor("be", [1, D], F32, kind="ExternalInput").ap()
    sel_in = nc.dram_tensor("sel", [8 * n_cores, 8], F32, kind="ExternalInput").ap()
    y_out = nc.dram_tensor("y", [NT, D], F32, kind="ExternalOutput").ap()

    dbg = "ExternalOutput" if debug else "Internal"
    xbf_h = nc.dram_tensor("xbf_h", [NT, D], BF16, kind="Internal").ap()
    wbf_h = {k: nc.dram_tensor(f"w{k}_h", [D, D], BF16, kind="Internal").ap()
             for k in ("q", "k", "v", "o", "e")}
    kfT_h = nc.dram_tensor("kfT_h", [D, NT], BF16, kind=dbg).ap()
    ao_h = nc.dram_tensor("ao_h", [NT, D], BF16, kind=dbg).ap()
    x1_h = nc.dram_tensor("x1_h", [NT, D], F32, kind=dbg).ap()
    x1hi_h = nc.dram_tensor("x1hi_h", [NT, D], BF16, kind="Internal").ap()
    x1lo_h = nc.dram_tensor("x1lo_h", [NT, D], BF16, kind="Internal").ap()
    bcol_h = nc.dram_tensor("bcol_h", [128, 128], BF16, kind=dbg).ap()
    gcol_h = nc.dram_tensor("gcol_h", [128, 128], BF16, kind=dbg).ap()
    btok_h = nc.dram_tensor("btok_h", [128, 128], BF16, kind=dbg).ap()
    gtok_h = nc.dram_tensor("gtok_h", [128, 128], BF16, kind="Internal").ap()
    g_h = nc.dram_tensor("g_h", [1, NT], F32, kind=dbg).ap()
    dbg_h = nc.dram_tensor("dbg_h", [128, 16 * 4 + 520 + 2048 + 128], F32, kind=dbg).ap() if debug else None
    cc_in = nc.dram_tensor("cc_in", [8, 1], F32, kind="Internal").ap()
    cc_out = nc.dram_tensor("cc_out", [8 * n_cores, 1], F32, kind="Internal").ap()

    es = contextlib.ExitStack()
    es1 = contextlib.ExitStack()
    E = es.enter_context
    E1 = es1.enter_context
    wT = {k: E(nc.sbuf_tensor(f"wT{k}", [128, 8 * D], BF16)) for k in ("q", "k", "v", "o")}
    wg_f = E(nc.sbuf_tensor("wg_f", [128, 8 * 8], F32))
    wg_hl = E(nc.sbuf_tensor("wg_hl", [128, 8 * 16], BF16))
    be_f = E(nc.sbuf_tensor("be_f", [1, D], F32))
    be_b = E(nc.sbuf_tensor("be_b", [1, D], BF16))
    sel_sb = E(nc.sbuf_tensor("sel_sb", [8 * n_cores, 8], F32))
    ones1b = E(nc.sbuf_tensor("ones1b", [1, 128], BF16))
    onescol = E(nc.sbuf_tensor("onescol", [128, 1], BF16))
    ones8 = E(nc.sbuf_tensor("ones8", [8, 1], F32))
    cst_eps = E(nc.sbuf_tensor("cst_eps", [128, 1], F32))
    cst_mh = E(nc.sbuf_tensor("cst_mh", [128, 1], F32))
    iota8 = E(nc.sbuf_tensor("iota8", [8, 1], F32))
    iota8r = E(nc.sbuf_tensor("iota8r", [128, 8], F32))
    iotap = E(nc.sbuf_tensor("iotap", [128, 1], F32))
    mask4 = E(nc.sbuf_tensor("mask4", [128, 512], BF16))
    recip = E(nc.sbuf_tensor("recip", [128, 16], F32))
    dtmp = E(nc.sbuf_tensor("dtmp", [128, 16], F32))
    jk2 = E(nc.sbuf_tensor("jk2", [128, 4], F32))
    vjk = E(nc.sbuf_tensor("vjk", [128, 1], F32))
    xres = [E(nc.sbuf_tensor(f"xres{j}", [128, D], F32)) for j in range(2)]
    ubuf = [E(nc.sbuf_tensor(f"ubuf{j}", [128, D], F32)) for j in range(2)]
    x1b = [E(nc.sbuf_tensor(f"x1b{j}", [128, D], F32)) for j in range(2)]
    x1hi = [E(nc.sbuf_tensor(f"x1hi{j}", [128, D], BF16)) for j in range(2)]
    x1lo = [E(nc.sbuf_tensor(f"x1lo{j}", [128, D], BF16)) for j in range(2)]
    stat = [E(nc.sbuf_tensor(f"stat{j}", [128, 12], F32)) for j in range(2)]
    mv = [E(nc.sbuf_tensor(f"mv{j}", [128, 2], F32)) for j in range(2)]
    lnt = [E(nc.sbuf_tensor(f"lnt{j}", [128, 4], F32)) for j in range(2)]
    hiT = E(nc.sbuf_tensor("hiT", [128, 8 * ST], BF16))
    loT = E(nc.sbuf_tensor("loT", [128, 8 * ST], BF16))
    lgb = E(nc.sbuf_tensor("lgb", [128, 8], F32))
    lge = E(nc.sbuf_tensor("lge", [128, 8], F32))
    ohb = E(nc.sbuf_tensor("ohb", [128, 8], F32))
    smt = E(nc.sbuf_tensor("smt", [128, 3], F32))
    gate_c = E(nc.sbuf_tensor("gate_c", [128, NTT], F32))
    best_c = E(nc.sbuf_tensor("best_c", [128, NTT], F32))
    gate_cb = E(nc.sbuf_tensor("gate_cb", [128, 128], BF16))
    best_cb = E(nc.sbuf_tensor("best_cb", [128, 128], BF16))
    xT = [E1(nc.sbuf_tensor(f"xT{j}", [128, 8 * ST], BF16)) for j in range(2)]
    qfT = E1(nc.sbuf_tensor("qfT", [128, 8 * ST], BF16))
    kfT = E1(nc.sbuf_tensor("kfT", [128, 8 * ST], BF16))
    vbuf = E1(nc.sbuf_tensor("vbuf", [128, 4 * 1040], BF16))
    vaug = vbuf.ap().rearrange("p (c h v) -> p c h v", h=16, v=65)
    kf = [E1(nc.sbuf_tensor(f"kf{j}", [128, D], BF16)) for j in range(2)]
    spE = E1(nc.sbuf_tensor("spE", [128, 1024], F32))
    spL = E1(nc.sbuf_tensor("spL", [128, 1024], F32))
    iotac = spE[:, 0:512]
    mask4f = spL[:, 0:512]
    amT = [E1(nc.sbuf_tensor(f"amT{j}", [128, 2048], BF16)) for j in range(2)]
    kv_sb = E1(nc.sbuf_tensor("kv_sb", [128, 520], BF16))
    aob = E1(nc.sbuf_tensor("aob", [128, 4 * D], BF16))
    aoT = E1(nc.sbuf_tensor("aoT", [128, 8 * ST], BF16))
    B = [E(nc.psum_tensor(f"pb{j}", [128, 512], F32)) for j in range(8)]

    p = Prog()
    S = {}

    for nm in ("GW", "LW", "GPRE", "PRE", "LXT0", "LXT1", "MMF", "BF2", "BF3",
               "BF4", "BF5", "BF6", "BF7", "AEX", "LG4",
               "VEQ", "GKF", "LKF0", "LKF1", "CA", "MEV", "NSC", "RCP", "DIV", "KVE",
               "GAO", "LAO", "LXR0", "LXR1", "VLN", "ALN", "GX1", "LHT", "VLG",
               "VTL", "GTL", "LTL", "CCD", "GP2", "LP2", "VP2", "AP2", "GY", "GDB", "DBV"):
        p.sem(nm)
    for b in range(BPC):
        p.sem(f"GWX{b}")

    bank = {2: {"frees": 0}, 3: {"frees": 0}, 6: {"frees": 0}, 7: {"frees": 0}}

    def g_pre(g):
        for k in ("q", "k", "v", "o", "e"):
            g.dma_start(out=wbf_h[k][:, :], in_=w_in[k][:, :]).then_inc(S["GW"], 16)
        for b in range(BPC):
            g.dma_start(out=xbf_h[b * T:(b + 1) * T, :],
                        in_=x_in[b * T:(b + 1) * T, :]).then_inc(S[f"GWX{b}"], 16)
        g.iota(iota8[:, :], pattern=[[0, 1]], channel_multiplier=1,
               allow_small_or_imprecise_dtypes=True)
        g.iota(iota8r[:, :], pattern=[[1, 8]], channel_multiplier=0,
               allow_small_or_imprecise_dtypes=True)
        g.iota(iotap[:, :], pattern=[[0, 1]], channel_multiplier=1,
               allow_small_or_imprecise_dtypes=True)
        g.iota(iotac, pattern=[[0, 4], [1, 128]], channel_multiplier=0,
               allow_small_or_imprecise_dtypes=True)
        g.memset(ones1b[:, :], 1.0)
        g.memset(onescol[:, :], 1.0)
        g.memset(vaug[:, :, :, 64:65], 1.0)
        g.memset(gate_cb[:, :], 0.0)
        g.memset(cst_eps[:, :], EPS_LN)
        g.memset(cst_mh[:, :], -0.5)
        g.memset(best_cb[:, :], 0.0)
        g.memset(ones8[:, :], 1.0).then_inc(S["GPRE"], 1)
    p.emit("gpsimd", g_pre)
    p.add("GW", 16 * 5)
    for b in range(BPC):
        p.add(f"GWX{b}", 16)
    p.add("GPRE", 1)

    def s_pre(sy):
        sy.wait_ge(S["GW"], 16 * 5)
        for k in ("q", "k", "v", "o"):
            for kk in range(8):
                sy.dma_start(out=wT[k][:, kk * D:(kk + 1) * D],
                             in_=wbf_h[k][:, kk * 128:(kk + 1) * 128],
                             transpose=True).then_inc(S["LW"], 16)
        sy.dma_start(out=wg_f.ap().rearrange("p (kk e) -> p kk e", kk=8),
                     in_=wg_in.rearrange("(kk p) e -> p kk e", p=128)).then_inc(S["LW"], 16)
        sy.dma_start(out=be_f[:, :], in_=be_in[:, :]).then_inc(S["LW"], 16)
        sy.dma_start(out=sel_sb[:, :], in_=sel_in[:, :]).then_inc(S["LW"], 16)
    p.emit("sync", s_pre)
    p.add("LW", 16 * 35)

    def v_pre(v):
        v.wait_ge(S["GPRE"], 1)
        v.wait_ge(S["LW"], 16 * 35)
        v.tensor_single_scalar(mask4f, iotac, iotap[:, :], ALU.is_ge)
        v.tensor_copy(out=vjk[:, :], in_=iotap[:, :])
        v.tensor_copy(out=mask4[:, :], in_=mask4f)
        for kk in range(8):
            v.tensor_copy(out=wg_hl[:, kk * 16:kk * 16 + 8], in_=wg_f[:, kk * 8:(kk + 1) * 8])
        v.tensor_copy(out=vjk[:, :], in_=iotap[:, :])
        for kk in range(8):
            v.tensor_sub(wg_hl[:, kk * 16 + 8:kk * 16 + 16], wg_f[:, kk * 8:(kk + 1) * 8],
                         wg_hl[:, kk * 16:kk * 16 + 8])
        v.tensor_copy(out=be_b[:, :], in_=be_f[:, :]).then_inc(S["PRE"], 1)
    p.emit("vector", v_pre)
    p.add("PRE", 1)

    lxt_after = {}
    xt_cons = {}
    veq_after = {}
    qk_n = [0]
    vfree_after = {}
    lkf_after = {}
    mev_after = {}
    nsc_after = {}
    rcp_after = {}
    div_after = {}
    mev_mid_t = {}
    kve_after = {}
    gao_after = {}
    lao_after = {}
    wo_cons = {}
    lxr_after = {}
    u_free = {}
    xres_free = {}
    x1_free = {}
    vlo_after = {}
    gx1_after = {}
    lht_after = {}
    lg_cons = {}
    lp2_st = {}
    lp2_x1 = {}
    eo_cons = {}
    gkf_prev = {}
    aln_lnexp = {}
    ap2_lnexp = {}
    p2_tile_free = {}
    vp2_st = {}
    gy_after = {}

    def stage_sHT(i):
        t0 = i * ST

        def s_ht(sy, i=i, t0=t0):
            sy.wait_ge(S["GX1"], gx1_after[i])
            if i >= 1:
                sy.wait_ge(S["MMF"], lg_cons[i - 1])
            for kk in range(8):
                sy.dma_start(out=hiT[:, kk * ST:(kk + 1) * ST],
                             in_=x1hi_h[t0:t0 + ST, kk * 128:(kk + 1) * 128],
                             transpose=True).then_inc(S["LHT"], 16)
                sy.dma_start(out=loT[:, kk * ST:(kk + 1) * ST],
                             in_=x1lo_h[t0:t0 + ST, kk * 128:(kk + 1) * 128],
                             transpose=True).then_inc(S["LHT"], 16)
        p.emit("sync", s_ht)
        lht_after[i] = p.add("LHT", 16 * 16)

    def stage_sXT(i):
        b = i // STB
        t0 = i * ST
        xTb = xT[i % 2]

        def s_xt(sy, i=i, b=b, t0=t0, xTb=xTb):
            sy.wait_ge(S[f"GWX{b}"], 16)
            if i >= 2:
                sy.wait_ge(S["MMF"], xt_cons[i - 2])
            for kk in range(8):
                sy.dma_start(out=xTb[:, kk * ST:(kk + 1) * ST],
                             in_=xbf_h[t0:t0 + ST, kk * 128:(kk + 1) * 128],
                             transpose=True).then_inc(S[f"LXT{i % 2}"], 16)
        p.emit("sync", s_xt)
        lxt_after[i] = p.add(f"LXT{i % 2}", 16 * 8)

    def stage_sAO(i):
        t0 = i * ST

        def s_ao(sy, i=i, t0=t0):
            sy.wait_ge(S["GAO"], gao_after[i])
            if i >= 1:
                sy.wait_ge(S["MMF"], wo_cons[i - 1])
            for kk in range(8):
                sy.dma_start(out=aoT[:, kk * ST:(kk + 1) * ST],
                             in_=ao_h[t0:t0 + ST, kk * 128:(kk + 1) * 128],
                             transpose=True).then_inc(S["LAO"], 16)
        p.emit("sync", s_ao)
        lao_after[i] = p.add("LAO", 16 * 8)

    def stage_fQ(i):
        b, stb = divmod(i, STB)
        t0 = i * ST
        xTb = xT[i % 2]

        if i <= 1:
            stage_sXT(i)
        if i >= 2:
            stage_sHT(i - 2)
        if i >= 1:
            stage_sAO(i - 1)

        prev_nsc = nsc_after.get(i * 4 - 1, 0)
        prev_gkf = gkf_prev.get(i - 1, 0)
        for (nm, dstT) in (("q", qfT), ("k", kfT)):
            for ee in range(8):
                bid = 2 + (ee % 2)
                bf = f"BF{bid}"
                w_bf = bank[bid]["frees"]
                my_fill = p.add("MMF", 1)

                def t_qk(t, nm=nm, ee=ee, bid=bid, w_bf=w_bf, xTb=xTb, i=i):
                    t.wait_ge(S[f"BF{bid}"], w_bf)
                    t.wait_ge(S[f"LXT{i % 2}"], lxt_after[i])
                    t.wait_ge(S["PRE"], 1)
                    for kk in range(8):
                        inst = t.matmul(B[bid][:, :],
                                        wT[nm][:, kk * D + ee * 128: kk * D + (ee + 1) * 128],
                                        xTb[:, kk * ST:(kk + 1) * ST],
                                        start=(kk == 0), stop=(kk == 7))
                    inst.then_inc(S["MMF"], 1)
                p.emit("tensor", t_qk)

                qk_n[0] += 1
                veq_wait = max(0, qk_n[0] - 2)
                sp0 = (qk_n[0] % 2) * 512
                my_aex = p.add("AEX", 1)

                def a_sp(s, my_fill=my_fill, bid=bid, sp0=sp0, veq_wait=veq_wait):
                    s.wait_ge(S["MMF"], my_fill)
                    if veq_wait > 0:
                        s.wait_ge(S["VEQ"], veq_wait)
                    s.activation(out=spE[:, sp0:sp0 + 512], in_=B[bid][:, :],
                                 func=AF.Exp).then_inc(S[f"BF{bid}"], 1)
                    s.activation(out=spL[:, sp0:sp0 + 512], in_=spE[:, sp0:sp0 + 512],
                                 func=AF.Ln, bias=1.0).then_inc(S["AEX"], 1)
                p.emit("scalar", a_sp)
                bank[bid]["frees"] += 1

                my_veq = p.add("VEQ", 1)

                def v_sp(v, my_aex=my_aex, dstT=dstT, ee=ee, sp0=sp0,
                         prev_nsc=prev_nsc, prev_gkf=prev_gkf, nm=nm):
                    v.wait_ge(S["AEX"], my_aex)
                    if ee == 0 and prev_nsc > 0:
                        v.wait_ge(S["NSC"], prev_nsc)
                        if nm == "k":
                            v.wait_ge(S["GKF"], prev_gkf)
                    v.tensor_scalar_add(dstT[:, ee * ST:(ee + 1) * ST],
                                        spL[:, sp0:sp0 + 512], EPS_ATTN).then_inc(S["VEQ"], 1)
                p.emit("vector", v_sp)
        veq_after[i] = p.cnt("VEQ")

        for tt in range(4):
            for half in range(2):
                bid = 2 + half
                w_bf = bank[bid]["frees"]
                my_fill = p.add("MMF", 1)

                def t_v(t, tt=tt, half=half, bid=bid, w_bf=w_bf, xTb=xTb, i=i):
                    t.wait_ge(S[f"BF{bid}"], w_bf)
                    t.wait_ge(S[f"LXT{i % 2}"], lxt_after[i])
                    for kk in range(8):
                        inst = t.matmul(B[bid][:, :],
                                        xTb[:, kk * ST + tt * 128: kk * ST + (tt + 1) * 128],
                                        wT["v"][:, kk * D + half * 512: kk * D + (half + 1) * 512],
                                        start=(kk == 0), stop=(kk == 7))
                    inst.then_inc(S["MMF"], 1)
                p.emit("tensor", t_v)

                def a_v(s, my_fill=my_fill, bid=bid, tt=tt, half=half,
                        prev_nsc=prev_nsc):
                    s.wait_ge(S["MMF"], my_fill)
                    if tt == 0 and half == 0 and prev_nsc > 0:
                        s.wait_ge(S["NSC"], prev_nsc)
                    s.activation(out=vaug[:, tt, half * 8:(half + 1) * 8, 0:64],
                                 in_=B[bid][:, :], func=AF.Copy).then_inc(S[f"BF{bid}"], 1)
                p.emit("scalar", a_v)
                bank[bid]["frees"] += 1
        xt_cons[i] = p.cnt("MMF")
        vfree_after[i] = (bank[2]["frees"], bank[3]["frees"])

        def g_kf(g, i=i, t0=t0):
            g.wait_ge(S["VEQ"], veq_after[i])
            g.dma_start(out=kfT_h.rearrange("(ee p) t -> p ee t", p=128)[:, :, t0:t0 + ST],
                        in_=kfT.ap().rearrange("p (ee t) -> p ee t", ee=8)).then_inc(S["GKF"], 16)
        p.emit("gpsimd", g_kf)
        gkf_after = p.add("GKF", 16)
        gkf_prev[i] = gkf_after

        for c in range(4):
            cc = i * 4 + c

            def s_kf(sy, i=i, c=c, cc=cc, t0=t0, gkf_after=gkf_after):
                sy.wait_ge(S["GKF"], gkf_after)
                if cc >= 2:
                    sy.wait_ge(S["NSC"], cc - 1)
                sy.dma_start(out=kf[c % 2][:, :],
                             in_=kfT_h[:, t0 + c * 128: t0 + (c + 1) * 128],
                             transpose=True).then_inc(S[f"LKF{c % 2}"], 16)
            p.emit("sync", s_kf)
            lkf_after[cc] = p.add(f"LKF{c % 2}", 16)

    def stage_fC(i):
        b, stb = divmod(i, STB)
        t0 = i * ST

        def core_at(c):
            cc = i * 4 + c
            amb = amT[c % 2]

            for hc in range(2):
                my_ca = p.add("CA", 1)
                mev_pre = p.cnt("MEV")

                def t_at(t, cc=cc, c=c, hc=hc, i=i, mev_pre=mev_pre):
                    t.wait_ge(S["VEQ"], veq_after[i])
                    for j, bk in enumerate((4, 5)):
                        if hc == 1:
                            t.wait_ge(S["MEV"], mev_pre - 1 + j)
                        elif cc >= 1:
                            t.wait_ge(S["MEV"], mev_after[cc - 1] - 1 + j)
                        for pp in range(4):
                            h_even = hc * 8 + 2 * pp
                            ee = h_even // 2
                            csl = slice(ee * ST + c * 128, ee * ST + (c + 1) * 128)
                            hp = slice(0, 64) if j == 0 else slice(64, 128)
                            inst = t.matmul(B[bk][:, pp * 128:(pp + 1) * 128],
                                            kfT[hp, csl], qfT[hp, csl],
                                            start=True, stop=True,
                                            tile_position=(j * 64, 0))
                    inst.then_inc(S["CA"], 1)
                p.emit("tensor", t_at)

                def v_me(v, my_ca=my_ca, hc=hc, cc=cc, amb=amb):
                    v.wait_ge(S["CA"], my_ca)
                    if cc >= 2:
                        v.wait_ge(S["NSC"], nsc_after[cc - 2])
                    for j, bk in enumerate((4, 5)):
                        base = hc * 1024 + j * 512
                        v.tensor_mul(amb[:, base:base + 512], B[bk][:, :],
                                     mask4[:, :]).then_inc(S["MEV"], 1)
                p.emit("vector", v_me)
                p.add("MEV", 2)
                if hc == 0:
                    mev_mid_t[cc] = p.cnt("MEV")
            mev_after[cc] = p.cnt("MEV")

        def core_ns(c):
            cc = i * 4 + c
            cb = stb * 4 + c
            first = (cb == 0)
            amb = amT[c % 2]

            my_nsc = p.add("NSC", 1)
            wbk = {bb: bank[bb]["frees"] for bb in (2, 3, 6, 7)}
            mpair = (2, 3) if c % 2 == 0 else (6, 7)
            spair = (6, 7) if c % 2 == 0 else (2, 3)

            def t_ns(t, cc=cc, c=c, i=i, first=first, cb=cb, wbk=dict(wbk),
                     mpair=mpair, spair=spair, amb=amb):
                t.wait_ge(S["MEV"], mev_mid_t[cc])
                for bb in mpair:
                    t.wait_ge(S[f"BF{bb}"], wbk[bb])
                t.wait_ge(S[f"LKF{c % 2}"], lkf_after[cc])
                if cc >= 1:
                    t.wait_ge(S["KVE"], kve_after[cc - 1])
                kfc = kf[c % 2]
                for h in list(range(14)) + [14, 15]:
                    ee = h // 2
                    odd = h % 2
                    if h == 8:
                        t.wait_ge(S["MEV"], mev_after[cc])
                    if h >= 14:
                        nb, off = spair[odd], 447
                        if h == 14:
                            for bb in spair:
                                t.wait_ge(S[f"BF{bb}"], wbk[bb])
                    else:
                        nb, off = mpair[odd], (h // 2) * 65
                    pr = h // 2
                    kvb = 0 if pr < 4 else 1
                    kvcol = (pr % 4) * 65
                    kvs = kvcol + 260 * kvb
                    kvpart = slice(0, 64) if odd == 0 else slice(64, 128)
                    tile_col = (0, 0) if odd == 0 else (0, 64)
                    tile_row = (0, 0) if odd == 0 else (64, 0)
                    qsl = slice(ee * ST + c * 128, ee * ST + (c + 1) * 128)
                    hpart = slice(0, 64) if odd == 0 else slice(64, 128)
                    ams = slice(am_off(h), am_off(h) + 128)
                    vsl = slice(c * 1040 + h * 65, c * 1040 + (h + 1) * 65)
                    opener = h in (0, 1, 14, 15)
                    closer = h in (12, 13, 14, 15)
                    if not first:
                        t.matmul(B[nb][:, off:off + 65], qfT[hpart, qsl],
                                 kv_sb[kvpart, kvs:kvs + 65],
                                 start=opener, stop=False, tile_position=tile_row)
                    t.matmul(B[nb][:, off:off + 65], amb[:, ams], vbuf[:, vsl],
                             start=(first and opener), stop=closer)
                    kv_open = first and h in (0, 1, 8, 9)
                    kv_close = (cb == CCH - 1) and h in (6, 7, 14, 15)
                    t.matmul(B[kvb][kvpart, kvcol:kvcol + 65], kfc[:, h * 64:(h + 1) * 64],
                             vbuf[:, vsl],
                             start=kv_open, stop=kv_close, tile_position=tile_col,
                             skip_group_check=True)
                inst = t.matmul(B[4][:, 300:428], amb[:, 0:128], kf[c % 2][:, 0:128],
                                start=True, stop=True)
                inst.then_inc(S["NSC"], 1)
            p.emit("tensor", t_ns)
            nsc_after[cc] = my_nsc

            my_rcp = p.add("RCP", 1)

            def v_rc(v, cc=cc, mpair=mpair, spair=spair):
                v.wait_ge(S["NSC"], nsc_after[cc])
                v.tensor_scalar_add(
                    dtmp[:, 0:7],
                    B[mpair[0]][:, 0:455].rearrange("p (n v) -> p n v", v=65)[:, :, 64:65],
                    EPS_ATTN)
                v.tensor_scalar_add(
                    dtmp[:, 7:14],
                    B[mpair[1]][:, 0:455].rearrange("p (n v) -> p n v", v=65)[:, :, 64:65],
                    EPS_ATTN)
                v.tensor_scalar_add(dtmp[:, 14:15], B[spair[0]][:, 511:512], EPS_ATTN)
                v.tensor_scalar_add(dtmp[:, 15:16], B[spair[1]][:, 511:512], EPS_ATTN)
                v.reciprocal(recip[:, 0:8], dtmp[:, 0:8])
                v.reciprocal(recip[:, 8:16], dtmp[:, 8:16]).then_inc(S["RCP"], 1)
            p.emit("vector", v_rc)
            rcp_after[cc] = my_rcp

            if debug and cc < 4:
                def g_dbg(g, cc=cc):
                    g.wait_ge(S["RCP"], rcp_after[cc])
                    g.dma_start(out=dbg_h[:, cc * 16:(cc + 1) * 16], in_=recip[:, :]).then_inc(S["GDB"], 16)
                p.emit("gpsimd", g_dbg)
                p.add("GDB", 16)
                gdb_now = p.cnt("GDB")

                def v_rc_gate(v, gdb_now=gdb_now):
                    v.wait_ge(S["GDB"], gdb_now)
                p.emit("vector", v_rc_gate)

            my_div = p.add("DIV", 1)
            prev_gao = gao_after.get(i - 1, 0)

            def a_dv(s, cc=cc, c=c, prev_gao=prev_gao, mpair=mpair, spair=spair):
                s.wait_ge(S["NSC"], nsc_after[cc])
                s.wait_ge(S["RCP"], rcp_after[cc])
                if c == 0 and prev_gao > 0:
                    s.wait_ge(S["GAO"], prev_gao)
                for h in (14, 15, 12, 13, 0, 1, 2, 3, 4, 5, 6, 7, 8, 9, 10, 11):
                    if h >= 14:
                        nb, off, rcol = spair[h % 2], 447, h
                    else:
                        nb = mpair[h % 2]
                        off = (h // 2) * 65
                        rcol = (h // 2) + 7 * (h % 2)
                    inst = s.activation(out=aob[:, c * D + h * 64: c * D + (h + 1) * 64],
                                        in_=B[nb][:, off:off + 64], func=AF.Copy,
                                        scale=recip[:, rcol:rcol + 1])
                    if h >= 14:
                        inst.then_inc(S[f"BF{spair[h % 2]}"], 1)
                inst.then_inc(S["DIV"], 1)
            p.emit("scalar", a_dv)
            div_after[cc] = my_div
            for bb in (2, 3, 6, 7):
                bank[bb]["frees"] += 1

            def a_dvf(s, mpair=mpair):
                for j, bb in enumerate(mpair):
                    s.activation(out=jk2[:, j:j + 1], in_=B[bb][:, 447:448],
                                 func=AF.Copy).then_inc(S[f"BF{bb}"], 1)
            p.emit("scalar", a_dvf)

            my_kve = p.add("KVE", 1)

            def v_kv(v, cc=cc):
                v.wait_ge(S["NSC"], nsc_after[cc])
                v.tensor_copy(out=kv_sb[:, 0:260], in_=B[0][:, 0:260])
                v.tensor_copy(out=kv_sb[:, 260:520], in_=B[1][:, 0:260]).then_inc(S["KVE"], 1)
            p.emit("vector", v_kv)
            kve_after[cc] = my_kve

            if debug and cc == 0:
                def g_dbg2(g):
                    g.wait_ge(S["KVE"], 1)
                    g.dma_start(out=dbg_h[:, 64:584], in_=kv_sb[:, :]).then_inc(S["GDB"], 16)
                    g.wait_ge(S["MEV"], 4)
                    g.dma_start(out=dbg_h[:, 584:584 + 2048], in_=amT[0][:, :]).then_inc(S["GDB"], 16)
                p.emit("gpsimd", g_dbg2)
                p.add("GDB", 32)
                gdb_now2 = p.cnt("GDB")

                def v_kv_gate(v, gdb_now2=gdb_now2):
                    v.wait_ge(S["GDB"], gdb_now2)
                p.emit("vector", v_kv_gate)

        for c in range(4):
            core_at(c)
            core_ns(c)

        def g_ao(g, i=i, t0=t0):
            g.wait_ge(S["DIV"], div_after[i * 4 + 3])
            for tt in range(4):
                g.dma_start(out=ao_h[t0 + tt * 128: t0 + (tt + 1) * 128, :],
                            in_=aob[:, tt * D:(tt + 1) * D]).then_inc(S["GAO"], 16)
        p.emit("gpsimd", g_ao)
        gao_after[i] = p.add("GAO", 16 * 4)

        if 2 <= i + 1 < NST:
            stage_sXT(i + 1)

    def stage_bW(i):
        t0 = i * ST

        for tt in range(4):
            tile = i * 4 + tt
            tb = tt % 2

            def s_xr(sy, i=i, t0=t0, tt=tt, tb=tb, tile=tile):
                if tile >= 2:
                    sy.wait_ge(S["VLN"], xres_free[tile - 2])
                sy.dma_start(out=xres[tb][:, :],
                             in_=x_in[t0 + tt * 128: t0 + (tt + 1) * 128, :]).then_inc(S[f"LXR{tb}"], 16)
            p.emit("sync", s_xr)
            lxr_after[tile] = p.add(f"LXR{tb}", 16)

            fills = []
            for half in range(2):
                bid = 2 + half
                w_bf = bank[bid]["frees"]
                my_fill = p.add("MMF", 1)
                fills.append(my_fill)

                def t_wo(t, tt=tt, half=half, bid=bid, w_bf=w_bf, i=i):
                    t.wait_ge(S[f"BF{bid}"], w_bf)
                    t.wait_ge(S["LAO"], lao_after[i])
                    for kk in range(8):
                        inst = t.matmul(B[bid][:, :],
                                        aoT[:, kk * ST + tt * 128: kk * ST + (tt + 1) * 128],
                                        wT["o"][:, kk * D + half * 512: kk * D + (half + 1) * 512],
                                        start=(kk == 0), stop=(kk == 7))
                    inst.then_inc(S["MMF"], 1)
                p.emit("tensor", t_wo)

            my_vln = p.add("VLN", 1)
            prev_aln = aln_lnexp.get(tile - 2, 0)

            def v_u(v, fills=tuple(fills), tb=tb, tile=tile, prev_aln=prev_aln):
                v.wait_ge(S["MMF"], fills[1])
                v.wait_ge(S[f"LXR{tb}"], lxr_after[tile])
                if prev_aln > 0:
                    v.wait_ge(S["ALN"], prev_aln)
                v.tensor_add(ubuf[tb][:, 0:512], xres[tb][:, 0:512], B[2][:, :]).then_inc(S["BF2"], 1)
                v.tensor_add(ubuf[tb][:, 512:1024], xres[tb][:, 512:1024], B[3][:, :]).then_inc(S["BF3"], 1)
                v.bn_stats(stat[tb][:, 0:6], ubuf[tb][:, 0:512])
                v.bn_stats(stat[tb][:, 6:12], ubuf[tb][:, 512:1024])
                v.tensor_copy(out=vjk[:, :], in_=iotap[:, :])
                v.bn_aggr(mv[tb][:, :], stat[tb][:, :]).then_inc(S["VLN"], 1)
            p.emit("vector", v_u)
            bank[2]["frees"] += 1
            bank[3]["frees"] += 1
            xres_free[tile] = my_vln

            my_aln = p.add("ALN", 1)
            aln_lnexp[tile] = my_aln

            def a_ln(s, my_vln=my_vln, tb=tb):
                s.wait_ge(S["VLN"], my_vln)
                s.activation(out=lnt[tb][:, 0:1], in_=mv[tb][:, 1:2], func=AF.Ln, bias=cst_eps[:, :])
                s.activation(out=lnt[tb][:, 1:2], in_=lnt[tb][:, 0:1], func=AF.Exp, scale=cst_mh[:, :])\
                 .then_inc(S["ALN"], 1)
            p.emit("scalar", a_ln)

            my_vln2 = p.add("VLN", 1)
            u_free[tile] = my_vln2

            def v_x1(v, my_aln=my_aln, tb=tb, tile=tile):
                v.wait_ge(S["ALN"], my_aln)
                if tile >= 2:
                    v.wait_ge(S["GX1"], x1_free.get(tile - 2, 0))
                v.tensor_scalar(x1b[tb][:, :], ubuf[tb][:, :], mv[tb][:, 0:1], lnt[tb][:, 1:2],
                                ALU.subtract, ALU.mult).then_inc(S["VLN"], 1)
            p.emit("vector", v_x1)

            my_aln2 = p.add("ALN", 1)

            def a_hi(s, my_vln2=my_vln2, tb=tb):
                s.wait_ge(S["VLN"], my_vln2)
                s.activation(out=x1hi[tb][:, :], in_=x1b[tb][:, :], func=AF.Copy)\
                 .then_inc(S["ALN"], 1)
            p.emit("scalar", a_hi)

            my_vln3 = p.add("VLN", 1)
            vlo_after[tile] = my_vln3

            def v_lo(v, my_aln2=my_aln2, tb=tb):
                v.wait_ge(S["ALN"], my_aln2)
                v.tensor_sub(x1lo[tb][:, :], x1b[tb][:, :], x1hi[tb][:, :]).then_inc(S["VLN"], 1)
            p.emit("vector", v_lo)

            gx1_pre = p.cnt("GX1")

            def g_x1(g, my_vln3=my_vln3, tb=tb, t0=t0, tt=tt, gx1_pre=gx1_pre):
                g.wait_ge(S["VLN"], my_vln3)
                g.wait_ge(S["GX1"], gx1_pre)
                r0 = t0 + tt * 128
                g.dma_start(out=x1_h[r0:r0 + 128, :], in_=x1b[tb][:, :]).then_inc(S["GX1"], 16)
                g.dma_start(out=x1hi_h[r0:r0 + 128, :], in_=x1hi[tb][:, :]).then_inc(S["GX1"], 16)
                g.dma_start(out=x1lo_h[r0:r0 + 128, :], in_=x1lo[tb][:, :]).then_inc(S["GX1"], 16)
            p.emit("gpsimd", g_x1)
            x1_free[tile] = p.add("GX1", 48)
        wo_cons[i] = p.cnt("MMF")
        gx1_after[i] = p.cnt("GX1")

    def stage_bL(i):
        for tt in range(4):
            col = i * 4 + tt
            bid = 2 + (tt % 2)
            w_bf = bank[bid]["frees"]
            my_fill = p.add("MMF", 1)

            def t_lg(t, tt=tt, i=i, w_bf=w_bf, bid=bid):
                t.wait_ge(S[f"BF{bid}"], w_bf)
                t.wait_ge(S["LHT"], lht_after[i])
                for kk in range(8):
                    tsl = slice(kk * ST + tt * 128, kk * ST + (tt + 1) * 128)
                    t.matmul(B[bid][:, 0:8], hiT[:, tsl], wg_hl[:, kk * 16:kk * 16 + 8],
                             start=(kk == 0), stop=False)
                    t.matmul(B[bid][:, 0:8], hiT[:, tsl], wg_hl[:, kk * 16 + 8:kk * 16 + 16],
                             start=False, stop=False)
                    inst = t.matmul(B[bid][:, 0:8], loT[:, tsl], wg_hl[:, kk * 16:kk * 16 + 8],
                                    start=False, stop=(kk == 7))
                inst = t.matmul(B[bid][:, 128:256], hiT[:, tsl], hiT[:, 0:128],
                                start=True, stop=True)
                inst.then_inc(S["MMF"], 1)
            p.emit("tensor", t_lg)

            my_vlg = p.add("VLG", 1)

            def v_lg(v, my_fill=my_fill, bid=bid):
                v.wait_ge(S["MMF"], my_fill)
                v.reduce_max(smt[:, 0:1], B[bid][:, 0:8], axis=AX.X, negate=True)
                v.reduce_max(smt[:, 2:3], B[bid][:, 0:8], axis=AX.X).then_inc(S["VLG"], 1)
            p.emit("vector", v_lg)

            my_aln3 = p.add("ALN", 1)

            def a_sm(s, my_vlg=my_vlg, bid=bid):
                s.wait_ge(S["VLG"], my_vlg)
                s.activation(out=lge[:, :], in_=B[bid][:, 0:8], func=AF.Exp,
                             bias=smt[:, 0:1], accum_out=smt[:, 1:2]).then_inc(S["ALN"], 1)
            p.emit("scalar", a_sm)

            def v_gb(v, my_aln3=my_aln3, col=col, bid=bid):
                v.wait_ge(S["ALN"], my_aln3)
                v.reciprocal(gate_c[:, col:col + 1], smt[:, 1:2])
                v.tensor_single_scalar(ohb[:, :], B[bid][:, 0:8], smt[:, 2:3],
                                       ALU.is_equal).then_inc(S[f"BF{bid}"], 1)
                v.tensor_copy(out=vjk[:, :], in_=iotap[:, :])
                v.scalar_tensor_tensor(lge[:, :], ohb[:, :], 1.0, iota8r[:, :],
                                       ALU.mult, ALU.mult,
                                       accum_out=best_c[:, col:col + 1]).then_inc(S["VLG"], 1)
            p.emit("vector", v_gb)
            bank[bid]["frees"] += 1
            p.add("VLG", 1)
        lg_cons[i] = p.cnt("MMF")

    for i in range(NST):
        stage_fQ(i)
        if i >= 2:
            stage_bL(i - 2)
        stage_fC(i)
        if i >= 1:
            stage_bW(i - 1)
    stage_sAO(NST - 1)
    stage_sHT(NST - 2)
    stage_bW(NST - 1)
    stage_bL(NST - 2)
    stage_sHT(NST - 1)
    stage_bL(NST - 1)

    es1.close()
    trb = E(nc.sbuf_tensor("trb", [128, 128], BF16))
    brep = [E(nc.sbuf_tensor(f"brep{j}", [8, FCH], BF16)) for j in range(2)]
    bufM = E(nc.sbuf_tensor("bufM", [8, FCH], F32))
    bufS = E(nc.sbuf_tensor("bufS", [8, FCH], F32))
    bufC = [E(nc.sbuf_tensor(f"bufC{j}", [8, FCH], F32)) for j in range(2)]
    run8 = E(nc.sbuf_tensor("run8", [8, 1], F32))
    tot8 = E(nc.sbuf_tensor("tot8", [8, NFC], F32))
    carry8 = E(nc.sbuf_tensor("carry8", [8, 1], F32))
    ag_sb = E(nc.sbuf_tensor("ag_sb", [8 * n_cores, 1], F32))
    cntf = [E(nc.sbuf_tensor(f"cntf{j}", [1, FCH], F32)) for j in range(2)]
    gatef = [E(nc.sbuf_tensor(f"gatef{j}", [1, FCH], BF16)) for j in range(2)]
    gf = [E(nc.sbuf_tensor(f"gf{j}", [1, FCH], F32)) for j in range(2)]
    g4 = [E(nc.sbuf_tensor(f"g4_{j}", [128, 4], F32)) for j in range(2)]
    we_sb = E(nc.sbuf_tensor("we_sb", [128, 8 * D], BF16))
    hiT2 = E(nc.sbuf_tensor("hiT2", [128, 8 * ST], BF16))
    hiP = [hiT, hiT2, loT]
    lg4_st = {}

    def stage_p2load(i):
        t0 = i * ST
        hib = hiP[i % 3]

        def s_p2(sy, i=i, t0=t0, hib=hib):
            if i < 3:
                sy.wait_ge(S["MMF"], lg_cons[NST - 1])
            else:
                sy.wait_ge(S["MMF"], eo_cons[i - 3])
            for kk in range(8):
                sy.dma_start(out=hib[:, kk * ST:(kk + 1) * ST],
                             in_=x1hi_h[t0:t0 + ST, kk * 128:(kk + 1) * 128],
                             transpose=True).then_inc(S["LP2"], 16)
        p.emit("sync", s_p2)
        lp2_st[i] = p.add("LP2", 16 * 8)

    def emit_s_we():
        def s_we(sy):
            sy.wait_ge(S["MMF"], lg_cons[NST - 1])
            sy.dma_start(out=we_sb.ap().rearrange("p (kk e) -> p kk e", kk=8),
                         in_=wbf_h["e"].rearrange("(kk p) e -> p kk e", p=128)).then_inc(S["LP2"], 16)
        p.emit("sync", s_we)
        return p.add("LP2", 16)

    vlg_all = p.cnt("VLG")

    def v_cols(v):
        v.wait_ge(S["VLG"], vlg_all)
        v.tensor_copy(out=vjk[:, :], in_=iotap[:, :])
        v.tensor_copy(out=gate_cb[:, 0:NTT], in_=gate_c[:, :])
        v.tensor_copy(out=best_cb[:, 0:NTT], in_=best_c[:, :]).then_inc(S["VTL"], 1)
    p.emit("vector", v_cols)
    p.add("VTL", 1)

    def g_cols(g):
        g.wait_ge(S["VTL"], 1)
        g.memset(run8[:, :], 0.0)
        g.dma_start(out=bcol_h[:, :], in_=best_cb[:, :]).then_inc(S["GTL"], 16)
        g.dma_start(out=gcol_h[:, :], in_=gate_cb[:, :]).then_inc(S["GTL"], 16)
    p.emit("gpsimd", g_cols)
    p.add("GTL", 32)

    def s_tr(sy):
        sy.wait_ge(S["GTL"], 32)
        sy.dma_start(out=trb[:, :], in_=bcol_h[:, :], transpose=True).then_inc(S["LTL"], 16)
    p.emit("sync", s_tr)
    p.add("LTL", 16)

    def g_tok(g):
        g.wait_ge(S["LTL"], 16)
        g.dma_start(out=btok_h[:, :], in_=trb[:, :]).then_inc(S["GTL"], 16)
    p.emit("gpsimd", g_tok)
    p.add("GTL", 16)

    def s_tr2(sy):
        sy.wait_ge(S["GTL"], 48)
        sy.dma_start(out=trb[:, :], in_=gcol_h[:, :], transpose=True).then_inc(S["LTL"], 16)
    p.emit("sync", s_tr2)
    p.add("LTL", 16)

    def g_tok2(g):
        g.wait_ge(S["LTL"], 32)
        g.dma_start(out=gtok_h[:, :], in_=trb[:, :]).then_inc(S["GTL"], 16)
    p.emit("gpsimd", g_tok2)
    p.add("GTL", 16)
    gtl_tok = p.cnt("GTL")

    btok_flat = btok_h.rearrange("a b -> (a b)")
    gtok_flat = gtok_h.rearrange("a b -> (a b)")

    stage_p2load(0)

    vtl_eq1 = {}
    for k in range(NFC):
        w_vtl = vtl_eq1.get(k - 2, 0)

        def s_br(sy, k=k, w_vtl=w_vtl):
            sy.wait_ge(S["GTL"], gtl_tok)
            if w_vtl:
                sy.wait_ge(S["VTL"], w_vtl)
            sy.dma_start(out=brep[k % 2][:, :],
                         in_=btok_flat[k * FCH:(k + 1) * FCH]
                         .rearrange("(a t) -> a t", a=1).partition_broadcast(8)).then_inc(S["LTL"], 16)
        p.emit("sync", s_br)
        my_ltl = p.add("LTL", 16)

        def v_scan(v, my_ltl=my_ltl, k=k):
            v.wait_ge(S["LTL"], my_ltl)
            v.wait_ge(S["GPRE"], 1)
            v.tensor_single_scalar(bufM[:, :], brep[k % 2][:, :], iota8[:, :],
                                   ALU.is_equal).then_inc(S["VTL"], 1)
            v.tensor_copy(out=vjk[:, :], in_=iotap[:, :])
            v.reduce_sum(tot8[:, k:k + 1], bufM[:, :], axis=AX.X).then_inc(S["VTL"], 1)
        p.emit("vector", v_scan)
        vtl_eq1[k] = p.add("VTL", 1)
        p.add("VTL", 1)

    def v_tot(v):
        v.tensor_copy(out=vjk[:, :], in_=iotap[:, :])
        v.reduce_sum(run8[:, :], tot8[:, :], axis=AX.X).then_inc(S["VTL"], 1)
    p.emit("vector", v_tot)
    p.add("VTL", 1)
    vtl_sweep1 = p.cnt("VTL")

    gtl_ccin = p.cnt("GTL") + 16

    def g_cc(g):
        g.wait_ge(S["VTL"], vtl_sweep1)
        g.dma_start(out=cc_in[:, :], in_=run8[:, :]).then_inc(S["GTL"], 16)
        g.wait_ge(S["GTL"], gtl_ccin)
        g.collective_compute("AllGather", ALU.bypass, ins=[cc_in[:, :]], outs=[cc_out[:, :]],
                             replica_groups=[list(range(n_cores))]).then_inc(S["CCD"], 1)
    p.emit("gpsimd", g_cc)
    p.add("GTL", 16)
    p.add("CCD", 1)

    def s_ag(sy):
        sy.wait_ge(S["CCD"], 1)
        sy.dma_start(out=ag_sb[:, :], in_=cc_out[:, :]).then_inc(S["LTL"], 16)
    p.emit("sync", s_ag)
    ltl_ag = p.add("LTL", 16)

    lp2_we = emit_s_we()
    if NST > 1:
        stage_p2load(1)
    if NST > 2:
        stage_p2load(2)

    carry_fill = p.add("MMF", 1)

    def t_carry(t):
        t.wait_ge(S["LTL"], ltl_ag)
        t.matmul(B[4][0:8, 0:1], sel_sb[:, :], ag_sb[:, :], start=True, stop=True)
        t.matmul(B[4][0:1, 64:192], ones8[:, :], bufM[:, 0:128],
                 start=True, stop=True).then_inc(S["MMF"], 1)
    p.emit("tensor", t_carry)

    my_vtl = p.add("VTL", 1)

    def v_carry(v):
        v.wait_ge(S["MMF"], carry_fill)
        v.tensor_copy(out=carry8[:, :], in_=B[4][0:8, 0:1]).then_inc(S["BF4"], 1)
        v.tensor_copy(out=vjk[:, :], in_=iotap[:, :])
        v.tensor_copy(out=run8[:, :], in_=carry8[:, :]).then_inc(S["VTL"], 1)
    p.emit("vector", v_carry)

    vtl_eq2 = {}
    vg_vtl = {}
    sumfill = {}
    gg_gp2 = {}
    for k in range(NFC):
        w_vtl = vtl_eq2.get(k - 2, 0)
        w_vg = vg_vtl.get(k - 2, 0)

        def s_br2(sy, k=k, w_vtl=w_vtl, w_vg=w_vg):
            sy.wait_ge(S["VTL"], max(w_vtl, w_vg, vtl_eq1[NFC - 1] if k < 2 else 0))
            sy.dma_start(out=brep[k % 2][:, :],
                         in_=btok_flat[k * FCH:(k + 1) * FCH]
                         .rearrange("(a t) -> a t", a=1).partition_broadcast(8)).then_inc(S["LTL"], 16)
            sy.dma_start(out=gatef[k % 2][:, :],
                         in_=gtok_flat[k * FCH:(k + 1) * FCH]
                         .rearrange("(a t) -> a t", a=1)).then_inc(S["LTL"], 16)
        p.emit("sync", s_br2)
        my_ltl = p.add("LTL", 32)

        def v_scan2(v, my_ltl=my_ltl, k=k, w_sum=sumfill.get(k - 2, 0)):
            v.wait_ge(S["LTL"], my_ltl)
            v.tensor_single_scalar(bufM[:, :], brep[k % 2][:, :], iota8[:, :],
                                   ALU.is_equal).then_inc(S["VTL"], 1)
            v.tensor_copy(out=vjk[:, :], in_=iotap[:, :])
            v.tensor_tensor_scan(out=bufS[:, :], data0=bufM[:, :], data1=bufM[:, :],
                                 initial=run8[:, :], op0=ALU.add, op1=ALU.bypass)
            v.tensor_copy(out=vjk[:, :], in_=iotap[:, :])
            v.tensor_copy(out=run8[:, :], in_=bufS[:, FCH - 1:FCH]).then_inc(S["VTL"], 1)
            if w_sum:
                v.wait_ge(S["MMF"], w_sum)
            v.tensor_mul(bufC[k % 2][:, :], bufM[:, :], bufS[:, :]).then_inc(S["VTL"], 1)
        p.emit("vector", v_scan2)
        p.add("VTL", 3)
        vtl_eq2[k] = p.cnt("VTL") - 2
        my_vtl2 = p.cnt("VTL")

        for j in range(FCH // 512):
            bk = 4 + j
            my_fill = p.add("MMF", 1)

            def t_sum(t, j=j, k=k, bk=bk, my_vtl2=my_vtl2):
                t.wait_ge(S[f"BF{bk}"], (1 if bk == 4 else 0) + max(0, k))
                t.wait_ge(S["VTL"], my_vtl2)
                t.matmul(B[bk][0:1, :], ones8[:, :], bufC[k % 2][:, j * 512:(j + 1) * 512],
                         start=True, stop=True).then_inc(S["MMF"], 1)
            p.emit("tensor", t_sum)

            def v_cf(v, my_fill=my_fill, j=j, k=k, bk=bk):
                v.wait_ge(S["MMF"], my_fill)
                v.tensor_copy(out=cntf[k % 2][:, j * 512:(j + 1) * 512],
                              in_=B[bk][0:1, :]).then_inc(S[f"BF{bk}"], 1)
            p.emit("vector", v_cf)
        sumfill[k] = p.cnt("MMF")

        my_vtl3 = p.add("VTL", 1)

        def v_g(v, k=k, w_gp2=gg_gp2.get(k - 2, 0)):
            if w_gp2:
                v.wait_ge(S["GP2"], w_gp2)
            v.tensor_copy(out=vjk[:, :], in_=iotap[:, :])
            v.tensor_single_scalar(bufM[0:1, :], cntf[k % 2][:, :], float(CAP) + 0.49, ALU.is_le)
            v.tensor_copy(out=vjk[:, :], in_=iotap[:, :])
            v.tensor_mul(gf[k % 2][:, :], bufM[0:1, :], gatef[k % 2][:, :]).then_inc(S["VTL"], 1)
        p.emit("vector", v_g)
        vg_vtl[k] = my_vtl3

        def g_g(g, k=k, my_vtl3=my_vtl3):
            g.wait_ge(S["VTL"], my_vtl3)
            g.dma_start(out=g_h[0:1, k * FCH:(k + 1) * FCH], in_=gf[k % 2][:, :]).then_inc(S["GP2"], 16)
        p.emit("gpsimd", g_g)
        gg_gp2[k] = p.add("GP2", 16)
    gp2_all = p.cnt("GP2")

    for i in range(NST):
        t0 = i * ST
        hib = hiP[i % 3]

        def s_g4(sy, i=i, t0=t0):
            sy.wait_ge(S["GP2"], 16 * (i // 2 + 1))
            if i >= 2:
                sy.wait_ge(S["VP2"], vp2_st[i - 2])
            sy.dma_start(out=g4[i % 2][:, :],
                         in_=g_h[0, t0:t0 + ST].rearrange("(tt p) -> p tt", p=128),
                         allow_slow_non_contiguous=True).then_inc(S["LG4"], 16)
        p.emit("sync", s_g4)
        lg4_st[i] = p.add("LG4", 16)

        for tt in range(4):
            tile = i * 4 + tt
            tb = tt % 2
            pr = (2, 3) if tile % 2 == 0 else (6, 7)

            def s_x1r(sy, i=i, t0=t0, tt=tt, tb=tb, tile=tile):
                if tile >= 2:
                    sy.wait_ge(S["VP2"], p2_tile_free[tile - 2])
                sy.dma_start(out=xres[tb][:, :],
                             in_=x1_h[t0 + tt * 128: t0 + (tt + 1) * 128, :]).then_inc(S[f"LXR{tb}"], 16)
            p.emit("sync", s_x1r)
            lp2_x1[tile] = p.add(f"LXR{tb}", 16)

            fills = []
            for half in range(2):
                bid = pr[half]
                w_bf = bank[bid]["frees"]
                my_fill = p.add("MMF", 1)
                fills.append(my_fill)

                def t_eo(t, tt=tt, half=half, bid=bid, w_bf=w_bf, i=i, hib=hib):
                    t.wait_ge(S[f"BF{bid}"], w_bf)
                    t.wait_ge(S["LP2"], max(lp2_st[i], lp2_we))
                    for kk in range(8):
                        t.matmul(B[bid][:, :],
                                 hib[:, kk * ST + tt * 128: kk * ST + (tt + 1) * 128],
                                 we_sb[:, kk * D + half * 512: kk * D + (half + 1) * 512],
                                 start=(kk == 0), stop=False)
                    inst = t.matmul(B[bid][:, :], ones1b[:, :],
                                    be_b[0:1, half * 512:(half + 1) * 512],
                                    start=False, stop=True)
                    inst.then_inc(S["MMF"], 1)
                p.emit("tensor", t_eo)

            my_vp2 = p.add("VP2", 1)
            prev_ap2 = ap2_lnexp.get(tile - 2, 0)

            def v_u2(v, fills=tuple(fills), tb=tb, tile=tile, tt=tt, i=i,
                     prev_ap2=prev_ap2, pr=pr):
                v.wait_ge(S["MMF"], fills[1])
                v.wait_ge(S["LG4"], lg4_st[i])
                if prev_ap2 > 0:
                    v.wait_ge(S["AP2"], prev_ap2)
                v.wait_ge(S[f"LXR{tb}"], lp2_x1[tile])
                v.scalar_tensor_tensor(ubuf[tb][:, 0:512], B[pr[0]][:, :], g4[i % 2][:, tt:tt + 1],
                                       xres[tb][:, 0:512], ALU.mult, ALU.add).then_inc(S[f"BF{pr[0]}"], 1)
                v.scalar_tensor_tensor(ubuf[tb][:, 512:1024], B[pr[1]][:, :], g4[i % 2][:, tt:tt + 1],
                                       xres[tb][:, 512:1024], ALU.mult, ALU.add).then_inc(S[f"BF{pr[1]}"], 1)
                v.bn_stats(stat[tb][:, 0:6], ubuf[tb][:, 0:512])
                v.bn_stats(stat[tb][:, 6:12], ubuf[tb][:, 512:1024])
                v.tensor_copy(out=vjk[:, :], in_=iotap[:, :])
                v.bn_aggr(mv[tb][:, :], stat[tb][:, :]).then_inc(S["VP2"], 1)
            p.emit("vector", v_u2)
            bank[pr[0]]["frees"] += 1
            bank[pr[1]]["frees"] += 1

            my_ap2 = p.add("AP2", 1)
            ap2_lnexp[tile] = my_ap2

            def a_ln2(s, my_vp2=my_vp2, tb=tb):
                s.wait_ge(S["VP2"], my_vp2)
                s.activation(out=lnt[tb][:, 0:1], in_=mv[tb][:, 1:2], func=AF.Ln, bias=cst_eps[:, :])
                s.activation(out=lnt[tb][:, 1:2], in_=lnt[tb][:, 0:1], func=AF.Exp, scale=cst_mh[:, :])\
                 .then_inc(S["AP2"], 1)
            p.emit("scalar", a_ln2)

            my_vp22 = p.add("VP2", 1)
            p2_tile_free[tile] = my_vp22

            def v_y(v, my_ap2=my_ap2, tb=tb, tile=tile):
                v.wait_ge(S["AP2"], my_ap2)
                if tile >= 2:
                    v.wait_ge(S["GY"], gy_after.get(tile - 2, 0))
                v.tensor_scalar(x1b[tb][:, :], ubuf[tb][:, :], mv[tb][:, 0:1], lnt[tb][:, 1:2],
                                ALU.subtract, ALU.mult).then_inc(S["VP2"], 1)
            p.emit("vector", v_y)

            gy_pre = p.cnt("GY")

            def g_y(g, my_vp22=my_vp22, tb=tb, t0=t0, tt=tt, gy_pre=gy_pre):
                g.wait_ge(S["VP2"], my_vp22)
                g.wait_ge(S["GY"], gy_pre)
                g.dma_start(out=y_out[t0 + tt * 128: t0 + (tt + 1) * 128, :],
                            in_=x1b[tb][:, :]).then_inc(S["GY"], 16)
            p.emit("gpsimd", g_y)
            gy_after[tile] = p.add("GY", 16)
        vp2_st[i] = p.cnt("VP2")
        eo_cons[i] = p.cnt("MMF")
        if i + 3 < NST:
            stage_p2load(i + 3)

    gy_all = p.cnt("GY")

    def s_fin(sy):
        sy.wait_ge(S["GY"], gy_all)
    p.emit("sync", s_fin)

    for name in p.counts:
        S[name] = es.enter_context(nc.semaphore(name))
    with nc.Block() as block:
        for eng_name in Prog.ENGS:
            def mk(eng_name):
                def body(eng):
                    for fn in p.items[eng_name]:
                        fn(eng)
                return body
            getattr(block, eng_name)(mk(eng_name))
    es.close()
    return nc


def shard_inputs(inputs, n_cores, bpc):
    x = np.ascontiguousarray(np.asarray(inputs["L0_attn_x"], dtype=np.float32))
    Bfull = x.shape[0]
    x2 = x.reshape(Bfull * T, D)
    maps = []
    for c in range(n_cores):
        sel = np.zeros((8 * n_cores, 8), np.float32)
        for r in range(c):
            sel[r * 8:(r + 1) * 8] = np.eye(8, dtype=np.float32)
        maps.append({
            "x": np.ascontiguousarray(x2[c * bpc * T:(c + 1) * bpc * T]),
            "wq": np.ascontiguousarray(np.asarray(inputs["L0_attn_Wq"], np.float32)),
            "wk": np.ascontiguousarray(np.asarray(inputs["L0_attn_Wk"], np.float32)),
            "wv": np.ascontiguousarray(np.asarray(inputs["L0_attn_Wv"], np.float32)),
            "wo": np.ascontiguousarray(np.asarray(inputs["L0_attn_Wo"], np.float32)),
            "we": np.ascontiguousarray(np.asarray(inputs["L0_ffn_w_expert"], np.float32)),
            "wg": np.ascontiguousarray(np.asarray(inputs["L0_ffn_w_gate"], np.float32)),
            "be": np.ascontiguousarray(np.asarray(inputs["L0_ffn_b_expert"], np.float32).reshape(1, D)),
            "sel": sel,
        })
    return maps


N_CORES = 8
BPC = 4
_CACHE = {}


def kernel(**inputs):
    if "nc" not in _CACHE:
        nc = bass.Bass("TRN2", target_bir_lowering=False, debug=False,
                       detect_race_conditions=False)
        build_program(nc, BPC, N_CORES)
        _CACHE["nc"] = nc
    nc = _CACHE["nc"]
    from concourse.bass_utils import run_bass_kernel_spmd
    maps = shard_inputs(inputs, N_CORES, BPC)
    res = run_bass_kernel_spmd(nc, maps, core_ids=list(range(N_CORES)))
    y = np.concatenate([np.asarray(res.results[c]["y"]) for c in range(N_CORES)], axis=0)
    return y.reshape(32, T, D)
